# version 05
# baseline (speedup 1.0000x reference)
import numpy as np
import ml_dtypes
import concourse.bass as bass
import concourse.mybir as mybir
from concourse.bass_utils import run_bass_kernel_spmd
from contextlib import ExitStack

F32 = mybir.dt.float32
BF16 = mybir.dt.bfloat16
I32 = mybir.dt.int32
AF = mybir.ActivationFunctionType
ALU = mybir.AluOpType
AX = mybir.AxisListType
BF = ml_dtypes.bfloat16

N = 2048
D = 1024
NT = 16
IC = 8
NH = 16
DK = 64
PKH = 8
DIMKEY = 512
DI = 2048
NE = 16
EI = 512

ALL_STAGES = ("ln0", "attn0", "ffn0", "ln1", "attn1", "moe")


class Prog:

    ENG = {"T": "tensor", "V": "vector", "S": "scalar", "P": "gpsimd", "Y": "sync"}
    ESEM = {"T": "sT", "V": "sV", "S": "sS", "P": "sP"}
    DSEM = {"Y": "dY", "P": "dP", "S": "dS"}

    def __init__(self, nc, es):
        self.nc = nc
        self.cmds = {e: [] for e in self.ENG}
        self.sem = {}
        for name in ("sT", "sV", "sS", "sP", "dY", "dP", "dS", "dO"):
            self.sem[name] = es.enter_context(nc.semaphore(name))
        self.cnt = {k: 0 for k in self.sem}
        self.lastw = {e: {} for e in self.ENG}

    def wait(self, eng, tok):
        if tok is None:
            return
        sem, val = tok
        if val <= 0 or self.lastw[eng].get(sem, 0) >= val:
            return
        self.lastw[eng][sem] = val
        h = self.sem[sem]
        self.cmds[eng].append(lambda E, h=h, v=val: E.wait_ge(h, v))

    def waits(self, eng, toks):
        for t in toks:
            self.wait(eng, t)

    def op(self, eng, fn, waits=()):
        for t in waits:
            self.wait(eng, t)
        sem = self.ESEM[eng]
        h = self.sem[sem]
        self.cnt[sem] += 1
        v = self.cnt[sem]
        self.cmds[eng].append(lambda E, fn=fn, h=h: fn(E).then_inc(h, 1))
        return (sem, v)

    def dma(self, eng, out, in_, waits=(), sem=None, slow=False):
        for t in waits:
            self.wait(eng, t)
        sem = sem or self.DSEM[eng]
        h = self.sem[sem]
        self.cnt[sem] += 16
        v = self.cnt[sem]
        self.cmds[eng].append(
            lambda E, o=out, i=in_, h=h, s=slow: E.dma_start(
                out=o, in_=i, allow_slow_non_contiguous=s).then_inc(h, 16))
        return (sem, v)

    def gather(self, out, table, idx_ap, waits=()):
        for t in waits:
            self.wait("P", t)
        h = self.sem["dP"]
        self.cnt["dP"] += 16
        v = self.cnt["dP"]
        self.cmds["P"].append(
            lambda E, o=out, tb=table, ix=idx_ap, h=h: E.indirect_dma_start(
                out=o, out_offset=None, in_=tb,
                in_offset=bass.IndirectOffsetOnAxis(ap=ix, axis=0),
            ).then_inc(h, 16))
        return ("dP", v)

    def now(self):
        return [(s, self.cnt[s]) for s in
                ("sT", "sV", "sS", "sP", "dY", "dP", "dS") if self.cnt[s] > 0]

    def barrier(self, engines="TVSPY"):
        toks = self.now()
        for e in engines:
            self.waits(e, toks)

    def finalize(self, block):
        for e, attr in self.ENG.items():
            cmds = self.cmds[e]
            if not cmds:
                continue
            getattr(block, attr)(lambda E, cmds=cmds: [c(E) for c in cmds])


def rope_perm():
    perm = []
    for g in range(4):
        for h in range(4 * g, 4 * g + 4):
            perm.extend(h * DK + 2 * i for i in range(DK // 2))
        for h in range(4 * g, 4 * g + 4):
            perm.extend(h * DK + 2 * i + 1 for i in range(DK // 2))
    return np.array(perm)


def host_prep(inp):
    d = {}
    perm = rope_perm()
    d["l0_wqT"] = np.ascontiguousarray(inp["L0_attn_Wq"][perm].T)
    d["l0_wkT"] = np.ascontiguousarray(inp["L0_attn_Wk"][perm].T)
    d["l0_wvT"] = np.ascontiguousarray(inp["L0_attn_Wv"].T)
    d["l0_woT"] = np.ascontiguousarray(inp["L0_attn_Wo"].T)
    d["l0f_wqT"] = np.ascontiguousarray(inp["L0_ffn_Wq"].T)
    kt = np.ascontiguousarray(np.transpose(inp["L0_ffn_keys"], (2, 0, 3, 1)))
    d["l0f_keysT"] = np.ascontiguousarray(kt.reshape(2 * 8, 4, 128, 128))
    d["l0f_winT"] = np.ascontiguousarray(inp["L0_ffn_W_in"].T)
    d["l0f_woutT"] = np.ascontiguousarray(inp["L0_ffn_W_out"].T)
    d["t_in"] = np.ascontiguousarray(
        np.concatenate([inp["L0_ffn_in_a"], inp["L0_ffn_in_b"]], axis=1))
    d["t_out"] = np.ascontiguousarray(
        np.concatenate([inp["L0_ffn_out_a"], inp["L0_ffn_out_b"]], axis=1))
    d["l1_wqT"] = np.ascontiguousarray(inp["L1_attn_Wq"].T)
    d["l1_wkT"] = np.ascontiguousarray(inp["L1_attn_Wk"].T)
    d["l1_wvT"] = np.ascontiguousarray(inp["L1_attn_Wv"].T)
    d["l1_woT"] = np.ascontiguousarray(inp["L1_attn_Wo"].T)
    d["l1_bq"] = np.ascontiguousarray(inp["L1_attn_bq"])
    d["l1_bk"] = np.ascontiguousarray(inp["L1_attn_bk"])
    d["l1_bv"] = np.ascontiguousarray(inp["L1_attn_bv"]).reshape(1, D)
    d["l1_bo"] = np.ascontiguousarray(inp["L1_attn_bo"]).reshape(1, D)
    d["l1_relb"] = np.ascontiguousarray(inp["L1_attn_rel_bias"].T).reshape(1, 16 * 31)
    d["l1_relb_rev"] = np.ascontiguousarray(
        inp["L1_attn_rel_bias"][::-1].T).reshape(1, 16 * 31)
    d["l1_gwT"] = np.ascontiguousarray(inp["L1_ffn_gate_weight"].T)
    d["l1_ebias"] = np.ascontiguousarray(inp["L1_ffn_e_bias"]).reshape(1, NE)
    d["l1_gpT"] = np.ascontiguousarray(np.transpose(inp["L1_ffn_gate_proj"], (0, 2, 1)))
    d["l1_upT"] = np.ascontiguousarray(np.transpose(inp["L1_ffn_up_proj"], (0, 2, 1)))
    d["l1_dpT"] = np.ascontiguousarray(np.transpose(inp["L1_ffn_down_proj"], (0, 2, 1)))

    inv = 10000.0 ** (-np.arange(32, dtype=np.float64) * (2.0 / DK))
    ang = np.arange(N, dtype=np.float64)[None, :] * inv[:, None]
    d["c_cos"] = np.tile(np.cos(ang), (4, 1)).astype(BF)
    d["c_sin"] = np.tile(np.sin(ang), (4, 1)).astype(BF)
    d["c_ident"] = np.eye(128, dtype=BF)
    kk = np.arange(128)[:, None]
    qq = np.arange(128)[None, :]
    d["c_tril"] = (kk <= qq).astype(BF)
    d["c_riota"] = np.broadcast_to(
        127.0 - np.arange(128, dtype=np.float32), (128, 128)).copy()
    d["c_ones"] = np.ones((1, 128), dtype=np.float32)
    d["c_ridx"] = (127 - np.arange(128)).astype(np.int32)
    return d


WEIGHT_SPECS = {
    "x": (N, D),
    "l0_wqT": (D, D), "l0_wkT": (D, D), "l0_wvT": (D, D), "l0_woT": (D, D),
    "l0f_wqT": (D, 8192), "l0f_keysT": (16, 4, 128, 128),
    "l0f_winT": (D, DI), "l0f_woutT": (DI, D),
    "t_in": (16384, 3072), "t_out": (16384, 3072),
    "l1_wqT": (D, D), "l1_wkT": (D, D), "l1_wvT": (D, D), "l1_woT": (D, D),
    "l1_bq": (D,), "l1_bk": (D,), "l1_bv": (1, D), "l1_bo": (1, D),
    "l1_relb": (1, 496), "l1_relb_rev": (1, 496),
    "l1_gwT": (D, NE), "l1_ebias": (1, NE),
    "l1_gpT": (NE, D, EI), "l1_upT": (NE, D, EI), "l1_dpT": (NE, EI, D),
    "c_riota": (128, 128), "c_ones": (1, 128), "c_ridx": (128,),
}
BF16_INPUTS = {"c_cos": (128, N), "c_sin": (128, N), "c_ident": (128, 128),
               "c_tril": (128, 128)}



def chunked_src(ap, col0, full_w, w, nchunk, rowblk=128):
    return bass.AP(ap.tensor, ap.offset + col0,
                   [[full_w, rowblk], [rowblk * full_w, nchunk], [1, w]])

def emit_ln(p, t, c, layer):
    nc = p.nc
    x_res, xnT = t["x_res"], t["xnT"]
    ls = t["lnscratch"]
    xt = t["ln_xtile"][:]
    dump = t["dump"]
    ev_tok = None
    tp_toks = [None, None]
    for nt in range(NT):
        xs = x_res[:, nt * D:(nt + 1) * D]
        s_ = ls[:, 0:1]; ss = ls[:, 1:2]; mean = ls[:, 2:3]; m2 = ls[:, 3:4]
        ve = ls[:, 4:5]; sd = ls[:, 5:6]; rstd = ls[:, 6:7]; nmr = ls[:, 7:8]
        t1 = p.op("V", lambda V, xs=xs, o=s_: V.reduce_sum(o, xs, axis=AX.X))
        t2 = p.op("S", lambda S, xs=xs, o=ss, dmp=dump[:, 0:D]: S.activation(
            dmp, xs, AF.Square, accum_out=o))
        t3 = p.op("V", lambda V, o=mean, i=s_: V.tensor_scalar(
            o, i, 1.0 / D, None, ALU.mult), waits=[t1])
        t4 = p.op("V", lambda V, o=m2, i=mean: V.tensor_tensor(
            o, i, i, ALU.mult), waits=[t3])
        t5 = p.op("V", lambda V, o=ve, i=ss, j=m2: V.scalar_tensor_tensor(
            o, i, 1.0 / D, j, ALU.mult, ALU.subtract), waits=[t2, t4])
        t5b = p.op("V", lambda V, o=ve, i=ve: V.tensor_scalar(
            o, i, 1e-5, None, ALU.add), waits=[t5])
        t6 = p.op("S", lambda S, o=sd, i=ve: S.activation(
            o, i, AF.Sqrt), waits=[t5b])
        t7 = p.op("V", lambda V, o=rstd, i=sd: V.reciprocal(o, i), waits=[t6])
        t8 = p.op("V", lambda V, o=nmr, i=mean, j=rstd: V.scalar_tensor_tensor(
            o, i, -1.0, j, ALU.mult, ALU.mult), waits=[t7])
        t9 = p.op("S", lambda S, o=xt, xs=xs, sc=rstd, b=nmr: S.activation(
            o, xs, AF.Identity, scale=sc, bias=b), waits=[t8])
        bank = t["pt"][nt % 2]
        tts = []
        for ic2 in range(IC):
            w = [t9, tp_toks[nt % 2]] if ic2 == 0 else []
            tts.append(p.op("T", lambda T, i=xt[:, ic2 * 128:(ic2 + 1) * 128],
                            o=bank[:, ic2 * 128:(ic2 + 1) * 128],
                            idn=t["ident"][:]: T.transpose(o, i, idn), waits=w))
        out_ap = bass.AP(xnT, nt * 128, [[IC * N, 128], [N, IC], [1, 128]])
        in_ap = bank[:, 0:IC * 128]
        tp_toks[nt % 2] = p.op("V", lambda V, o=out_ap, i=in_ap: V.tensor_copy(o, i),
                               waits=[tts[-1]])
    p.barrier()


def emit_attn_core(p, t, c, layer):
    nc = p.nc
    l1 = layer == 1
    xnT = t["xnT"]
    wg = t["wg"]
    if not l1:
        qe, qo, tmpb = t["qe"], t["qo"], t["tmpb"]
    else:
        qe = qo = tmpb = None
    qrot, krot = t["qrot"], t["krot"]
    vg, att = t["vg"], t["att"]
    pbufs = [t["pbuf0"], t["pbuf1"]]
    unn = t["dump"]
    rr = t["rr"]
    ps = t["ps"]
    wname = "l1" if l1 else "l0"
    wq_h = t[f"{wname}_wqT"]; wk_h = t[f"{wname}_wkT"]
    wv_h = t[f"{wname}_wvT"]; wo_h = t[f"{wname}_woT"]

    bias_toks = []
    if l1:
        bq8, bkt = t["bq8"], t["bkt"]
        tb1 = p.dma("Y", bq8[:], t["l1_bq"].rearrange("(c p) -> p c", p=128), slow=True)
        tb2 = p.dma("Y", bkt[:], t["l1_bk"].rearrange("(c p) -> p c", p=128), slow=True)
        tb3 = p.dma("P", t["bvrow"][:], t["l1_bv"])
        tb4 = p.dma("P", t["borow"][:], t["l1_bo"])
        tb5 = p.op("V", lambda V, o=bq8[:], i=bq8[:]: V.tensor_scalar(
            o, i, 0.125, None, ALU.mult), waits=[tb1])
        bias_toks = [tb2, tb3, tb4, tb5]
        tb6 = p.dma("Y", t["relb"][:], t["l1_relb"])
        tb7 = p.dma("Y", t["relb_rev"][:], t["l1_relb_rev"])
        bias_toks += [tb6, tb7]

    use_tok = {}
    t["attn_use_tok"] = use_tok
    for g in range(4):
        osl = slice(g * 256, (g + 1) * 256)
        for which, wh in (("q", wq_h), ("k", wk_h)):
            wsb = wg
            lwt = p.dma("P", wsb[:], chunked_src(wh, osl.start, D, 256, IC),
                        waits=[use_tok.get("wg")])
            dst_e, dst_o = (qe, qo)
            rot = qrot if which == "q" else krot
            for oc2 in range(2):
                for pairq in range(2):
                    mmlast = []
                    for ncq in (2 * pairq, 2 * pairq + 1):
                        mms = []
                        for ic in range(IC):
                            first = ic == 0
                            w0 = [lwt, use_tok.get(("ps", ncq)),
                                  use_tok.get(("sps", ncq)),
                                  use_tok.get(("avps", ncq - 2))] if ic == 0 else []
                            mms.append(p.op("T", lambda T, o=ps[ncq][:, 0:512],
                                            l=wsb[:, ic * 256 + oc2 * 128: ic * 256 + (oc2 + 1) * 128],
                                            r=xnT[:, ic * N + ncq * 512: ic * N + (ncq + 1) * 512],
                                            st=first, sp=ic == IC - 1:
                                            T.matmul(o, l, r, start=st, stop=sp),
                                            waits=w0))
                        use_tok["wg"] = mms[-1]
                        mmlast.append(mms[-1])
                    dst = (dst_e if oc2 == 0 else dst_o) if not l1 else rot
                    dcol = pairq * 1024 if not l1 else (oc2 * N + pairq * 1024)
                    scale = 0.125 if which == "q" else 1.0
                    kw = {}
                    if l1:
                        bt = bq8 if which == "q" else bkt
                        kw["bias"] = bt[:, g * 2 + oc2: g * 2 + oc2 + 1]
                    ev = p.op("S", lambda S, o=dst[:, dcol:dcol + 1024],
                              i=t["psbig"][:, pairq * 1024:(pairq + 1) * 1024],
                              sc=scale, kw=kw:
                              S.activation(o, i, AF.Identity, scale=sc, **kw),
                              waits=mmlast + [use_tok.get("qeqo")]
                              + (bias_toks if l1 else []))
                    use_tok[("ps", 2 * pairq)] = ev
                    use_tok[("ps", 2 * pairq + 1)] = ev
            if not l1:
                evs = [use_tok[("ps", i)] for i in range(4)]
                m1 = p.op("V", lambda V, o=rot[:, 0:N], a=dst_e[:], cc=t["cos"][:]:
                          V.tensor_tensor(o, a, cc, ALU.mult),
                          waits=evs + [use_tok.get("rotbuf_" + which)])
                m2 = p.op("V", lambda V, o=tmpb[:], a=dst_o[:], cc=t["sin"][:]:
                          V.tensor_tensor(o, a, cc, ALU.mult))
                m3 = p.op("V", lambda V, o=rot[:, 0:N], a=rot[:, 0:N], b=tmpb[:]:
                          V.tensor_tensor(o, a, b, ALU.subtract), waits=[m1, m2])
                m4 = p.op("V", lambda V, o=rot[:, N:2 * N], a=dst_e[:], cc=t["sin"][:]:
                          V.tensor_tensor(o, a, cc, ALU.mult))
                m5 = p.op("V", lambda V, o=tmpb[:], a=dst_o[:], cc=t["cos"][:]:
                          V.tensor_tensor(o, a, cc, ALU.mult), waits=[m3])
                m6 = p.op("V", lambda V, o=rot[:, N:2 * N], a=rot[:, N:2 * N],
                          b=tmpb[:]: V.tensor_tensor(o, a, b, ALU.add),
                          waits=[m4, m5])
                use_tok[id(rot)] = m6
                use_tok["qeqo"] = m6
            else:
                use_tok[id(rot)] = use_tok[("ps", 3)]
        lwv = p.dma("P", wg[:], chunked_src(wv_h, osl.start, D, 256, IC),
                    waits=[use_tok.get("wg")])
        vz = p.op("V", lambda V, o=bass.AP(vg, 64, [[NT * 260, 128], [65, NT * 4]]):
                  V.memset(o, 1.0), waits=[use_tok.get("vg_done")])
        v_toks = []
        for nt in range(NT):
            mms = []
            for ic in range(IC):
                w0 = [lwv, use_tok.get(("ps", 4)), use_tok.get(("sps", 4)),
                      use_tok.get(("sps", 5)), use_tok.get(("rbps", 0)),
                      use_tok.get(("rbps", 1))] if ic == 0 else []
                mms.append(p.op("T", lambda T, o=ps[4][:, 0:256],
                                l=xnT[:, ic * N + nt * 128: ic * N + (nt + 1) * 128],
                                r=wg[:, ic * 256:(ic + 1) * 256],
                                st=ic == 0, sp=(ic == IC - 1) and not l1:
                                T.matmul(o, l, r, start=st, stop=sp),
                                waits=w0))
            use_tok["wg"] = mms[-1]
            if l1:
                mms.append(p.op("T", lambda T, o=ps[4][:, 0:256],
                                l=t["ones_bf"][:],
                                r=t["bvrow"][:, osl]:
                                T.matmul(o, l, r, start=False, stop=True)))
            out_ap = bass.AP(vg, nt * 260, [[NT * 260, 128], [65, 4], [1, 64]])
            in_ap = bass.AP(t["psbig"], 4 * 512, [[3072, 128], [64, 4], [1, 64]])
            ev = p.op("S", lambda S, o=out_ap, i=in_ap: S.activation(
                o, i, AF.Copy), waits=[mms[-1], vz])
            use_tok[("ps", 4)] = ev
            v_toks.append(ev)
        for hh in range(4):
            h = 4 * g + hh
            if not l1:
                base = hh * 32
                kparts = [(krot, 0, base), (krot, N, base)]
                qparts = [(qrot, 0, base), (qrot, N, base)]
            else:
                cidx = hh // 2
                base = (hh % 2) * 64
            if l1:
                frrow = t["frrow"]
                f1 = p.op("V", lambda V, o=frrow[:, 0:2033],
                          b=t["relb"][:, h * 31 + 30: h * 31 + 31]:
                          V.tensor_copy(o, b.to_broadcast([1, 2033])),
                          waits=[use_tok.get("frrow")] + bias_toks)
                f2 = p.op("V", lambda V, o=frrow[:, 2063:4095],
                          b=t["relb"][:, h * 31: h * 31 + 1]:
                          V.tensor_copy(o, b.to_broadcast([1, 2032])))
                f3 = p.op("V", lambda V, o=frrow[:, 2033:2063],
                          i=t["relb_rev"][:, h * 31 + 1: h * 31 + 31]:
                          V.tensor_copy(o, i))
                f4 = p.op("S", lambda S, o=frrow[:], i=frrow[:]:
                          S.activation(o, i, AF.Exp), waits=[f1, f2, f3])
                f5 = p.dma("Y", t["hbm_row"].rearrange("(a b) -> a b", a=1),
                           frrow[:], waits=[f4, use_tok.get("hbmrow")])
                use_tok["frrow"] = f5
                st_tok = p.gather(t["strip"][:], t["hbm_row"].rearrange(
                    "(a b) -> a b", b=1), t["ridx"][:],
                    waits=[f5, use_tok.get("attdone")])
                use_tok["hbmrow"] = st_tok
            for qh in range(2):
                ktmax = (qh + 1) * 8 if not l1 else NT
                av_toks = []
                expm_tok = {}
                smm_tok = {}
                avk_tok = {}

                def emit_S(kt):
                    pa = (0, 1) if kt % 2 == 0 else (4, 5)
                    smms = []
                    for qq in range(2):
                        w0 = [use_tok.get(("sps", pa[qq])),
                              use_tok.get(("ps", 4)) if pa[qq] >= 4 else None,
                              use_tok.get(("rbps", qq)) if pa[qq] >= 4 else None,
                              use_tok.get(id(qrot)), use_tok.get(id(krot))]
                        if not l1:
                            for ci, (kten, kcol, kb) in enumerate(kparts):
                                qten, qcol, _ = qparts[ci]
                                smms.append(p.op("T", lambda T,
                                    o=ps[pa[qq]][:, 0:512],
                                    l=kten[kb:kb + 32, kcol + kt * 128: kcol + (kt + 1) * 128],
                                    r=qten[kb:kb + 32, qcol + qh * 1024 + qq * 512: qcol + qh * 1024 + (qq + 1) * 512],
                                    st=ci == 0, sp=ci == 1, tp=(base, 0):
                                    T.matmul(o, l, r, start=st, stop=sp,
                                             tile_position=tp), waits=w0 if ci == 0 else []))
                        else:
                            smms.append(p.op("T", lambda T,
                                o=ps[pa[qq]][:, 0:512],
                                l=krot[base:base + 64, cidx * N + kt * 128: cidx * N + (kt + 1) * 128],
                                r=qrot[base:base + 64, cidx * N + qh * 1024 + qq * 512: cidx * N + qh * 1024 + (qq + 1) * 512],
                                tp=(base, 0):
                                T.matmul(o, l, r, start=True, stop=True,
                                         tile_position=tp), waits=w0))
                    smm_tok[kt] = smms[-1]

                def emit_exp(kt):
                    pa = (0, 1) if kt % 2 == 0 else (4, 5)
                    bcol = pa[0] * 512
                    pb = pbufs[kt % 2]
                    diag = (kt * 128 - qh * 1024) if not l1 else -(10 ** 9)
                    lo = max(diag, 0)
                    pfree = avk_tok.get(kt - 2, use_tok.get(("pbuf", kt % 2)))
                    ets = []
                    if lo > 0:
                        ets.append(p.op("V", lambda V, o=pb[:, 0:lo]:
                                        V.memset(o, 0.0), waits=[pfree]))
                    ets.append(p.op("S", lambda S, o=pb[:, lo:1024],
                                    i=t["psbig"][:, bcol + lo:bcol + 1024]:
                                    S.activation(o, i, AF.Exp),
                                    waits=[smm_tok[kt], pfree]))
                    if not l1 and 0 <= diag < 1024:
                        ets.append(p.op("V", lambda V, o=pb[:, lo:lo + 128],
                                        m=t["tril"][:]: V.tensor_tensor(
                                            o, o, m, ALU.mult), waits=[ets[-1]]))
                    if l1:
                        soff = qh * 1024 + 1920 - kt * 128
                        eng = "V" if kt % 2 == 0 else "P"
                        ets.append(p.op(eng, lambda E, o=pb[:],
                                        m=t["strip"][:, soff:soff + 1024]:
                                        E.tensor_tensor(o, o, m, ALU.mult),
                                        waits=ets + [st_tok]))
                    use_tok[("sps", pa[0])] = ets[-1]
                    use_tok[("sps", pa[1])] = ets[-1]
                    expm_tok[kt] = ets[-1]

                def emit_AV(kt):
                    pb = pbufs[kt % 2]
                    for qq in range(2):
                        av_toks.append(p.op("T", lambda T,
                            o=ps[2 + qq][0:65, 0:512],
                            l=vg[:, kt * 260 + hh * 65: kt * 260 + (hh + 1) * 65],
                            r=pb[:, qq * 512:(qq + 1) * 512],
                            st=kt == 0, sp=kt == ktmax - 1:
                            T.matmul(o, l, r, start=st, stop=sp),
                            waits=[expm_tok[kt], v_toks[-1],
                                   use_tok.get(("avps", qq))]))
                    avk_tok[kt] = av_toks[-1]

                for kt in range(ktmax):
                    emit_S(kt)
                    if kt >= 1:
                        emit_exp(kt - 1)
                        emit_AV(kt - 1)
                emit_exp(ktmax - 1)
                emit_AV(ktmax - 1)
                use_tok[("pbuf", 0)] = av_toks[-1]
                use_tok[("pbuf", 1)] = av_toks[-1]
                d1 = p.op("V", lambda V, o=rr[:, 0:512], i=ps[2][64:65, 0:512]:
                          V.tensor_copy(o, i),
                          waits=[av_toks[-1], av_toks[-2], use_tok.get("rr")])
                d2 = p.op("V", lambda V, o=rr[:, 512:1024], i=ps[3][64:65, 0:512]:
                          V.tensor_copy(o, i))
                d3 = p.op("V", lambda V, o=rr[:, 0:1024], i=rr[:, 0:1024]:
                          V.reciprocal(o, i), waits=[d1, d2])
                rbs = []
                for qq in range(2):
                    rbs.append(p.op("T", lambda T, o=ps[4 + qq][0:64, 0:512],
                                    l=t["ones"][:, 0:64],
                                    r=rr[:, qq * 512:(qq + 1) * 512]:
                                    T.matmul(o, l, r, start=True, stop=True),
                                    waits=[d3, use_tok.get(("rbps", qq))]))
                use_tok["rr"] = rbs[-1]
                uevs = []
                for qq in range(2):
                    uevs.append(p.op("S", lambda S, o=unn[0:64, qq * 512:(qq + 1) * 512],
                                     i=ps[2 + qq][0:64, 0:512]:
                                     S.activation(o, i, AF.Copy),
                                     waits=[av_toks[-1], av_toks[-2], d1, d2,
                                            use_tok.get(("unn", qq))]))
                use_tok[("avps", 0)] = uevs[0]
                use_tok[("avps", 1)] = uevs[1]
                acol = (h // 2) * N + qh * 1024
                arow = (h % 2) * 64
                nrm = []
                for qq in range(2):
                    nrm.append(p.op("V", lambda V,
                        o=att[arow:arow + 64, acol + qq * 512: acol + (qq + 1) * 512],
                        a=unn[0:64, qq * 512:(qq + 1) * 512],
                        b=ps[4 + qq][0:64, 0:512]:
                        V.tensor_tensor(o, a, b, ALU.mult),
                        waits=[uevs[qq], rbs[qq]]))
                use_tok[("rbps", 0)] = nrm[0]
                use_tok[("rbps", 1)] = nrm[1]
                use_tok[("unn", 0)] = nrm[0]
                use_tok[("unn", 1)] = nrm[1]
                use_tok[("attdone")] = nrm[-1]
        use_tok["vg_done"] = use_tok.get("attdone")
        use_tok["qeqo"] = use_tok.get("attdone")
        use_tok["rotbuf_q"] = use_tok.get("attdone")
        use_tok["rotbuf_k"] = use_tok.get("attdone")
        use_tok["wg"] = use_tok.get("attdone")


def emit_attn_proj(p, t, c, layer):
    l1 = layer == 1
    ps = t["ps"]
    att = t["att"]
    use_tok = t["attn_use_tok"]
    wo_h = t["l1_woT"] if l1 else t["l0_woT"]
    NTL = NT
    wo_sb = t["wo_sb"]
    att_done = use_tok.get("attdone")
    for oc in range(2):
        lwo = p.dma("P", wo_sb[:], chunked_src(wo_h, oc * 512, D, 512, IC),
                    waits=[att_done, use_tok.get("wo_free")])
        for nt in range(NT):
            mms = []
            if l1:
                mms.append(p.op("T", lambda T, o=ps[oc][:, 0:512],
                                l=t["ones_bf"][:],
                                r=t["borow"][:, oc * 512:(oc + 1) * 512]:
                                T.matmul(o, l, r, start=True, stop=False),
                                waits=[lwo, att_done, use_tok.get(("ops", oc))]))
            for ic in range(IC):
                first = (ic == 0) and not l1
                mms.append(p.op("T", lambda T, o=ps[oc][:, 0:512],
                                l=att[:, ic * N + nt * 128:ic * N + (nt + 1) * 128],
                                r=wo_sb[:, ic * 512:(ic + 1) * 512],
                                st=first, sp=ic == IC - 1:
                                T.matmul(o, l, r, start=st, stop=sp),
                                waits=([lwo, att_done, use_tok.get(("ops", oc))]
                                       if ic == 0 else [])))
            xa = t["x_res"][:, nt * D + oc * 512: nt * D + (oc + 1) * 512]
            ev = p.op("V", lambda V, o=xa, a=ps[oc][:, 0:512], b=xa:
                      V.tensor_tensor(o, a, b, ALU.add), waits=[mms[-1]])
            use_tok[("ops", oc)] = ev
            use_tok["wo_free"] = mms[-1]
    p.barrier()


def emit_ffn0_sel(p, t, c):
    nc = p.nc
    xnT = t["xnT"]
    ps = t["ps"]
    dump = t["dump"]
    keys_sb, wqf, qph = t["keys_sb"], t["wqf"], t["qph"]
    sim_sb, am, ar = t["sim_sb"][:], t["am"][:], t["ar"][:]
    mxA, mxB, ixA, ixB = t["mxA"][:], t["mxB"][:], t["ixA"][:], t["ixB"][:]
    gate, idxI = t["gate"][:], t["idxI"][:]
    lk = p.dma("P", keys_sb[:], bass.AP(
        t["l0f_keysT"].tensor, 0,
        [[128, 128], [4 * 128 * 128, 16], [128 * 128, 4], [1, 128]]))
    use = {}
    for ph in range(16):
        pp, h = divmod(ph, 8)
        wb = wqf[ph % 2]
        lwt = p.dma("P", wb[:], chunked_src(t["l0f_wqT"], ph * 512, 8192, 512, IC),
                    waits=[use.get(("wqf", ph % 2))])
        qlast = []
        for oc in range(4):
            for pairq in range(2):
                mml = []
                for ncq in (2 * pairq, 2 * pairq + 1):
                    mms = []
                    for ic in range(IC):
                        mms.append(p.op("T", lambda T, o=ps[ncq][:, 0:512],
                                        l=wb[:, ic * 512 + oc * 128: ic * 512 + (oc + 1) * 128],
                                        r=xnT[:, ic * N + ncq * 512:ic * N + (ncq + 1) * 512],
                                        st=ic == 0, sp=ic == IC - 1:
                                        T.matmul(o, l, r, start=st, stop=sp),
                                        waits=([lwt, use.get(("qps", pairq))]
                                               if ic == 0 else [])))
                    mml.append(mms[-1])
                ev = p.op("S", lambda S,
                    o=qph[:, oc * N + pairq * 1024: oc * N + (pairq + 1) * 1024],
                    i=t["psbig"][:, pairq * 1024:(pairq + 1) * 1024]:
                    S.activation(o, i, AF.Copy), waits=mml)
                use[("qps", pairq)] = ev
                qlast.append(ev)
        use[("wqf", ph % 2)] = qlast[-1]
        qph_done = qlast[-2:]
        mxX = mxA if pp == 0 else mxB
        ixX = ixA if pp == 0 else ixB
        for nt in range(NT):
            mms = []
            for dc in range(4):
                mms.append(p.op("T", lambda T, o=ps[4][:, 0:128],
                                l=qph[:, dc * N + nt * 128:dc * N + (nt + 1) * 128],
                                r=keys_sb[:, ph * 512 + dc * 128: ph * 512 + (dc + 1) * 128],
                                st=dc == 0, sp=dc == 3:
                                T.matmul(o, l, r, start=st, stop=sp),
                                waits=(qph_done + [lk, use.get("simev")]
                                       if dc == 0 else [])))
            sev = p.op("S", lambda S, o=sim_sb, i=ps[4][:, 0:128]:
                       S.activation(o, i, AF.Copy), waits=[mms[-1]])
            use["simev"] = sev
            a1 = p.op("V", lambda V, o=am, i=sim_sb: V.max(o, i), waits=[sev])
            a2 = p.op("V", lambda V, o=dump[:, 0:128], i=sim_sb, s=am[:, 0:1],
                      r=t["riota"][:]: V.scalar_tensor_tensor(
                          o, i, s, r, ALU.is_equal, ALU.mult),
                      waits=[a1, t["riota_tok"]])
            a3 = p.op("V", lambda V, o=ar, i=dump[:, 0:128]:
                      V.reduce_max(o, i, axis=AX.X), waits=[a2])
            a4 = p.op("V", lambda V, o=ixX[:, nt * 8 + h:nt * 8 + h + 1], i=ar:
                      V.tensor_scalar(o, i, -1.0, 127.0, ALU.mult, ALU.add),
                      waits=[a3])
            a5 = p.op("V", lambda V, o=mxX[:, nt * 8 + h:nt * 8 + h + 1],
                      i=am[:, 0:1]: V.tensor_copy(o, i), waits=[a1])
            use["simev"] = (a2[0], a2[1])
            use["lastsel"] = a5
    i1 = p.op("V", lambda V, o=gate, a=ixA, b=ixB: V.scalar_tensor_tensor(
        o, a, 128.0, b, ALU.mult, ALU.add),
        waits=[use["lastsel"]])
    i2 = p.op("V", lambda V, o=idxI, i=gate: V.tensor_copy(o, i), waits=[i1])
    i3 = p.op("V", lambda V, o=gate, a=mxA, b=mxB: V.tensor_tensor(
        o, a, b, ALU.add), waits=[i2])
    i4 = p.op("V", lambda V, o=gate, i=gate: V.tensor_scalar(
        o, i, 0.0, None, ALU.max), waits=[i3])
    t["sel_done"] = [i2, i4]
    p.barrier()


def emit_ffn0_main(p, t, c):
    xnT = t["xnT"]
    ps = t["ps"]
    dump = t["dump"]
    gate, idxI = t["gate"][:], t["idxI"][:]
    sel_done = t["sel_done"]
    win, wout = t["win"], t["wout"]
    gbufs = [t["gbuf0"], t["gbuf1"]]
    hids = [t["hid0"], t["hid1"]]
    hid_bf, hidT, xn_tile = t["hid_bf"], t["hidT"], t["xn_tile"]
    lin1, lg1 = t["lin1"][:], t["lg1"][:]
    lin2 = t["lin2"][:]
    lw1 = p.dma("P", win[:], chunked_src(t["l0f_winT"], 0, DI, DI, IC))
    lw2 = p.dma("P", wout[:], chunked_src(t["l0f_woutT"], 0, D, D, 16))
    use = {}
    st_nt = {}

    def emit_XT(nt):
        bank = t["pt"][0]
        tts = []
        for ic in range(IC):
            tts.append(p.op("T", lambda T, i=xnT[:, ic * N + nt * 128:ic * N + (nt + 1) * 128],
                            o=bank[:, ic * 128:(ic + 1) * 128], idn=t["ident"][:]:
                            T.transpose(o, i, idn),
                            waits=[use.get("pt0")] if ic == 0 else []))
        xt_ev = p.op("V", lambda V, o=xn_tile[:], i=bank[:, 0:D]:
                     V.tensor_copy(o, i),
                     waits=[tts[-1], use.get("xn_tile_free")])
        use["pt0"] = xt_ev
        st_nt[nt] = {"xt": xt_ev}

    def emit_WIN(nt):
        par = nt % 2
        hid = hids[par]
        hevs = []
        for pairq in range(2):
            mml = []
            for j in (2 * pairq, 2 * pairq + 1):
                mms = []
                for ic in range(IC):
                    mms.append(p.op("T", lambda T, o=ps[j][:, 0:512],
                                    l=xnT[:, ic * N + nt * 128:ic * N + (nt + 1) * 128],
                                    r=win[:, ic * DI + j * 512: ic * DI + (j + 1) * 512],
                                    st=ic == 0, sp=ic == IC - 1:
                                    T.matmul(o, l, r, start=st, stop=sp),
                                    waits=([lw1, use.get(("hps", pairq))]
                                           if ic == 0 else [])))
                mml.append(mms[-1])
            hevs.append(p.op("S", lambda S,
                o=hid[:, pairq * 1024:(pairq + 1) * 1024],
                i=t["psbig"][:, pairq * 1024:(pairq + 1) * 1024]:
                S.activation(o, i, AF.Copy),
                waits=mml + [use.get(("hid_free", par))]))
            use[("hps", pairq)] = hevs[-1]
        st_nt[nt]["hevs"] = hevs

    def emit_gather(nt, h, stage):
        gi = nt * 16 + (0 if stage == "A" else 8) + h
        par = gi % 2
        tok = p.gather(gbufs[par][:],
                       t["t_in"] if stage == "A" else t["t_out"],
                       idxI[:, nt * 8 + h:nt * 8 + h + 1],
                       waits=sel_done + [use.get(("gbuf", par))])
        st_nt[nt][("g", stage, h)] = (tok, par)

    def emit_EXA(nt, h):
        s = st_nt[nt]
        gt, par = s[("g", "A", h)]
        gbuf = gbufs[par]
        r1 = p.op("V", lambda V, o=dump[:, 0:D], i=xn_tile[:],
                  g2=gbuf[:, 0:D], acc=lin1:
                  V.scalar_tensor_tensor(o, i, 1.0, g2, ALU.bypass,
                                         ALU.mult, accum_out=acc),
                  waits=[gt, s["xt"]])
        r2 = p.op("V", lambda V, o=lg1, a=lin1,
                  b=gate[:, nt * 8 + h:nt * 8 + h + 1]:
                  V.tensor_tensor(o, a, b, ALU.mult), waits=[r1])
        hid = hids[nt % 2]
        r3 = p.op("V", lambda V, o=hid[:], i=gbuf[:, D:3072], s2=lg1, x=hid[:]:
                  V.scalar_tensor_tensor(o, i, s2, x, ALU.mult, ALU.add),
                  waits=[r2, s.get("exa")] + s["hevs"])
        use[("gbuf", par)] = r3
        s["exa"] = r3
        if h == 7:
            s["xa_last_r1"] = r1

    def emit_GELU(nt):
        s = st_nt[nt]
        par = nt % 2
        gl = p.op("S", lambda S, o=hid_bf[:], i=hids[par][:]:
                  S.activation(o, i, AF.Gelu),
                  waits=[s["exa"], use.get("hidbf_free"), use.get("hT_last_tt")])
        s["gl"] = gl
        use[("hid_free", par)] = gl
        use["xn_tile_free"] = s["xa_last_r1"]

    def emit_TR(nt):
        s = st_nt[nt]
        tevs = []
        for half in range(2):
            bank = t["pt"][half]
            bkey = "pt0" if half == 0 else "pt1"
            tts = []
            for ic2 in range(IC):
                icg = half * 8 + ic2
                tts.append(p.op("T", lambda T,
                                i=hid_bf[:, icg * 128:(icg + 1) * 128],
                                o=bank[:, ic2 * 128:(ic2 + 1) * 128],
                                idn=t["ident"][:]: T.transpose(o, i, idn),
                                waits=[s["gl"], use.get(bkey)] if ic2 == 0 else []))
            tevs.append(p.op("V", lambda V, o=hidT[:, half * 1024:(half + 1) * 1024],
                             i=bank[:, 0:1024]: V.tensor_copy(o, i),
                             waits=[tts[-1], use.get("hidT_free")]))
            use[bkey] = tevs[-1]
        use["hT_last_tt"] = tts[-1]
        s["tevs"] = tevs

    def emit_WOUT(nt):
        s = st_nt[nt]
        wmms = []
        for oc in range(2):
            mms = []
            for ic in range(16):
                mms.append(p.op("T", lambda T, o=ps[4 + oc][:, 0:512],
                                l=hidT[:, ic * 128:(ic + 1) * 128],
                                r=wout[:, ic * D + oc * 512: ic * D + (oc + 1) * 512],
                                st=ic == 0, sp=ic == 15:
                                T.matmul(o, l, r, start=st, stop=sp),
                                waits=([lw2, s["tevs"][0], s["tevs"][1],
                                        use.get("wps")] if ic == 0 else [])))
            wmms.append(mms[-1])
        xa = t["x_res"][:, nt * D:(nt + 1) * D]
        wp = p.op("V", lambda V, o=xa, a=t["psbig"][:, 2048:3072],
                  b=xa: V.tensor_tensor(o, a, b, ALU.add), waits=wmms)
        use["wps"] = wp
        use["hidT_free"] = wmms[-1]
        s["wps"] = [wp]

    def emit_EXB(nt, h):
        s = st_nt[nt]
        gt, par = s[("g", "B", h)]
        gbuf = gbufs[par]
        r1a = p.op("V", lambda V, o=dump[:, 0:D], i=hid_bf[:, 0:D],
                   g2=gbuf[:, 0:D], acc=lin1:
                   V.scalar_tensor_tensor(o, i, 1.0, g2, ALU.bypass,
                                          ALU.mult, accum_out=acc),
                   waits=[gt, s["gl"]])
        r1 = p.op("V", lambda V, o=dump[:, 0:D], i=hid_bf[:, D:DI],
                  g2=gbuf[:, D:DI], acc=lin2:
                  V.scalar_tensor_tensor(o, i, 1.0, g2, ALU.bypass,
                                         ALU.mult, accum_out=acc),
                  waits=[r1a])
        r2a = p.op("V", lambda V, o=lg1, a=lin1, b=lin2:
                   V.tensor_tensor(o, a, b, ALU.add), waits=[r1])
        r2 = p.op("V", lambda V, o=lg1, a=lg1,
                  b=gate[:, nt * 8 + h:nt * 8 + h + 1]:
                  V.tensor_tensor(o, a, b, ALU.mult), waits=[r2a])
        xa = t["x_res"][:, nt * D:(nt + 1) * D]
        r3 = p.op("V", lambda V, o=xa, i=gbuf[:, DI:3072], s2=lg1, x=xa:
                  V.scalar_tensor_tensor(o, i, s2, x, ALU.mult, ALU.add),
                  waits=[r2, s.get("exb")] + s["wps"])
        s["exb"] = r3
        use[("gbuf", par)] = r3
        if h == 7:
            use["hidbf_free"] = r1

    for nt in range(NT):
        emit_XT(nt)
        emit_WIN(nt)
        if nt >= 1:
            emit_TR(nt - 1)
            emit_WOUT(nt - 1)
            for h in range(PKH):
                emit_gather(nt - 1, h, "B")
                emit_EXB(nt - 1, h)
        for h in range(PKH):
            emit_gather(nt, h, "A")
            emit_EXA(nt, h)
        emit_GELU(nt)
    emit_TR(NT - 1)
    emit_WOUT(NT - 1)
    for h in range(PKH):
        emit_gather(NT - 1, h, "B")
        emit_EXB(NT - 1, h)
    p.barrier()


def emit_moe(p, t, c):
    nc = p.nc
    xnT, ps, dump = t["xnT"], t["ps"], t["dump"]
    gw_sb, ebb = t["gw_sb"], t["ebb"][:]
    scores, comb = t["scores"], t["comb"]
    m8r, mr16, gsc = t["m8r"][:], t["mr16"][:], t["gsc"][:]
    t16, t16b, gm, smask, wsel = (t["t16"][:], t["t16b"][:], t["gm"][:],
                                  t["smask"][:], t["wsel"][:])
    dn = t["dn"][:]
    lg = p.dma("P", gw_sb[:], chunked_src(t["l1_gwT"], 0, NE, NE, IC))
    le = p.dma("Y", t["ebrow"][:], t["l1_ebias"])
    eb1 = p.op("T", lambda T, o=ps[5][:, 0:16], l=t["ones"][:], r=t["ebrow"][:]:
               T.matmul(o, l, r, start=True, stop=True), waits=[le])
    eb2 = p.op("V", lambda V, o=ebb[:], i=ps[5][:, 0:16]: V.tensor_copy(o, i),
               waits=[eb1])
    use = {}
    for nt in range(NT):
        mms = []
        for ic in range(IC):
            mms.append(p.op("T", lambda T, o=ps[5][:, 0:16],
                            l=xnT[:, ic * N + nt * 128:ic * N + (nt + 1) * 128],
                            r=gw_sb[:, ic * 16:(ic + 1) * 16],
                            st=ic == 0, sp=ic == IC - 1:
                            T.matmul(o, l, r, start=st, stop=sp),
                            waits=([lg, eb2, use.get("rps")] if ic == 0 else [])))
        use["rps"] = p.op("S", lambda S, o=scores[:, nt * 16:(nt + 1) * 16],
                          i=ps[5][:, 0:16]: S.activation(o, i, AF.Sigmoid),
                          waits=[mms[-1]])
    gz = p.op("V", lambda V, o=gsc[:, 4:8]: V.memset(o, -1e30))
    for nt in range(NT):
        sc_nt = scores[:, nt * 16:(nt + 1) * 16]
        s1 = p.op("V", lambda V, o=t16, a=sc_nt, b=ebb[:]:
                  V.tensor_tensor(o, a, b, ALU.add), waits=[use["rps"], eb2])
        pairs = [(0, 1), (0, 2), (0, 3), (1, 2), (1, 3), (2, 3)]
        ptoks = []
        for k2, (e1, e2) in enumerate(pairs):
            a_ap = bass.AP(t["t16"], e1, [[16, 128], [4, 4]])
            b_ap = bass.AP(t["t16"], e2, [[16, 128], [4, 4]])
            if k2 == 0:
                ptoks.append(p.op("V", lambda V, o=gsc[:, 0:4], a=a_ap, b=b_ap:
                                  V.tensor_tensor(o, a, b, ALU.add), waits=[s1]))
            else:
                pr = p.op("V", lambda V, o=t["pr4"][:], a=a_ap, b=b_ap:
                          V.tensor_tensor(o, a, b, ALU.add), waits=[s1, ptoks[-1]])
                ptoks.append(p.op("V", lambda V, o=gsc[:, 0:4], a=gsc[:, 0:4],
                                  b=t["pr4"][:]: V.tensor_tensor(o, a, b, ALU.max),
                                  waits=[pr]))
        g1 = p.op("V", lambda V, o=m8r, i=gsc: V.max(o, i), waits=[ptoks[-1], gz])
        g2 = p.op("V", lambda V, o=m8r[:, 2:8]: V.memset(o, -1e30), waits=[g1])
        g3 = p.op("V", lambda V, o=mr16[:, 0:8], r=m8r, i=gsc:
                  V.match_replace(o, r, i, -1e30), waits=[g2])
        g4 = p.op("V", lambda V, o=gm, a=gsc[:, 0:4], b=mr16[:, 0:4]:
                  V.tensor_tensor(o, a, b, ALU.not_equal), waits=[g3])
        g5 = p.op("V", lambda V, o=bass.AP(t["smask"], 0, [[16, 128], [4, 4], [1, 4]]),
                  i=bass.AP(t["gm"], 0, [[4, 128], [1, 4], [0, 4]]):
                  V.tensor_copy(o, i), waits=[g4])
        m1 = p.op("V", lambda V, o=t16b, a=t16, b=smask:
                  V.tensor_tensor(o, a, b, ALU.mult), waits=[g5])
        m2 = p.op("V", lambda V, o=m8r, i=t16b: V.max(o, i), waits=[m1])
        m3 = p.op("V", lambda V, o=m8r[:, 4:8]: V.memset(o, -1e30), waits=[m2])
        m4 = p.op("V", lambda V, o=mr16, r=m8r, i=t16b:
                  V.match_replace(o, r, i, -1e30), waits=[m3])
        m5 = p.op("V", lambda V, o=smask, a=t16b, b=mr16:
                  V.tensor_tensor(o, a, b, ALU.not_equal), waits=[m4])
        w1 = p.op("V", lambda V, o=wsel, a=smask, b=sc_nt:
                  V.tensor_tensor(o, a, b, ALU.mult), waits=[m5])
        w2 = p.op("V", lambda V, o=dn, i=wsel: V.reduce_sum(o, i, axis=AX.X),
                  waits=[w1])
        w3 = p.op("V", lambda V, o=dn, i=dn: V.tensor_scalar(
            o, i, 1e-20, None, ALU.add), waits=[w2])
        w4 = p.op("V", lambda V, o=dn, i=dn: V.reciprocal(o, i), waits=[w3])
        p.op("V", lambda V, o=comb[:, nt * 16:(nt + 1) * 16], a=wsel, s=dn:
             V.tensor_scalar(o, a, s, None, ALU.mult), waits=[w4])
    p.barrier()
    gp_sb, up_sb, dp_sb = t["gp_sb"], t["up_sb"], t["dp_sb"]
    act, stmp = t["act"], t["stmp"]
    use = {}
    for e in range(NE):
        par = e % 2
        l1t = p.dma("P", gp_sb[par][:],
                    chunked_src(t["l1_gpT"][e], 0, EI, EI, IC),
                    waits=[use.get(("gp", par))])
        l2t = p.dma("P", up_sb[par][:],
                    chunked_src(t["l1_upT"][e], 0, EI, EI, IC),
                    waits=[use.get(("up", par))])
        l3t = p.dma("P", dp_sb[par][:],
                    chunked_src(t["l1_dpT"][e], 0, D, D, 4),
                    waits=[use.get(("dp", par))])
        act_toks = []
        for oc in range(4):
            for nh in range(2):
                gmm = []
                umm = []
                for j in range(2):
                    for ic in range(IC):
                        gmm.append(p.op("T", lambda T, o=ps[j][:, 0:512],
                            l=gp_sb[par][:, ic * 512 + oc * 128: ic * 512 + (oc + 1) * 128],
                            r=xnT[:, ic * N + nh * 1024 + j * 512: ic * N + nh * 1024 + (j + 1) * 512],
                            st=ic == 0, sp=ic == IC - 1:
                            T.matmul(o, l, r, start=st, stop=sp),
                            waits=([l1t, use.get(("gps", j))] if ic == 0 else [])))
                    for ic in range(IC):
                        umm.append(p.op("T", lambda T, o=ps[2 + j][:, 0:512],
                            l=up_sb[par][:, ic * 512 + oc * 128: ic * 512 + (oc + 1) * 128],
                            r=xnT[:, ic * N + nh * 1024 + j * 512: ic * N + nh * 1024 + (j + 1) * 512],
                            st=ic == 0, sp=ic == IC - 1:
                            T.matmul(o, l, r, start=st, stop=sp),
                            waits=([l2t, use.get(("ups", j))] if ic == 0 else [])))
                for j in range(2):
                    sl = p.op("S", lambda S, o=stmp[:, j * 512:(j + 1) * 512],
                              i=ps[j][:, 0:512]:
                              S.activation(o, i, AF.Silu),
                              waits=[gmm[8 * j + 7], umm[8 * j + 7],
                                     use.get(("stmp", j))])
                    av = p.op("V", lambda V,
                              o=act[:, oc * N + nh * 1024 + j * 512: oc * N + nh * 1024 + (j + 1) * 512],
                              a=stmp[:, j * 512:(j + 1) * 512], b=ps[2 + j][:, 0:512]:
                              V.tensor_tensor(o, a, b, ALU.mult),
                              waits=[sl, use.get("actfree")])
                    use[("gps", j)] = av
                    use[("ups", j)] = av
                    use[("stmp", j)] = av
                    act_toks.append(av)
        for nt in range(NT):
            dmml = []
            for oc2 in range(2):
                mms = []
                for ic in range(4):
                    mms.append(p.op("T", lambda T, o=ps[4 + oc2][:, 0:512],
                        l=act[:, ic * N + nt * 128:ic * N + (nt + 1) * 128],
                        r=dp_sb[par][:, ic * D + oc2 * 512: ic * D + (oc2 + 1) * 512],
                        st=ic == 0, sp=ic == 3:
                        T.matmul(o, l, r, start=st, stop=sp),
                        waits=([l3t, use.get("dps")] + act_toks
                               if ic == 0 else [])))
                dmml.append(mms[-1])
            xa = t["x_res"][:, nt * D:(nt + 1) * D]
            ev = p.op("V", lambda V, o=xa, i=t["psbig"][:, 2048:3072],
                      s=comb[:, nt * 16 + e:nt * 16 + e + 1], x=xa:
                      V.scalar_tensor_tensor(o, i, s, x, ALU.mult, ALU.add),
                      waits=dmml + [use.get("xres")])
            use["dps"] = ev
            use["actfree"] = dmml[-1]
        use["xres"] = ev
        use[("gp", par)] = ev
        use[("up", par)] = ev
        use[("dp", par)] = ev
    p.barrier()


def build_nc(stages=ALL_STAGES, dbg=()):
    nc = bass.Bass("TRN2", target_bir_lowering=False)
    t = {}
    for name, shape in WEIGHT_SPECS.items():
        t[name] = nc.dram_tensor(name, list(shape), I32 if name == "c_ridx" else F32,
                                 kind="ExternalInput").ap()
    for name, shape in BF16_INPUTS.items():
        t[name] = nc.dram_tensor(name, list(shape), BF16, kind="ExternalInput").ap()
    out = nc.dram_tensor("out", [N, D], F32, kind="ExternalOutput").ap()
    t["hbm_row"] = nc.dram_tensor("hbm_row", [4095], BF16, kind="Internal").ap()
    dbg_out = {}
    for d in dbg:
        if d == "xnT":
            dbg_out[d] = nc.dram_tensor("dbg_xnT", [128, IC * N], BF16,
                                        kind="ExternalOutput").ap()
        if d == "att1":
            dbg_out["att1"] = nc.dram_tensor("dbg_att", [128, IC * N], BF16,
                                             kind="ExternalOutput").ap()
            dbg_out["strip1"] = nc.dram_tensor("dbg_strip", [128, 3968], BF16,
                                               kind="ExternalOutput").ap()
    t["dbg_out"] = dbg_out

    es = ExitStack()
    p = Prog(nc, es)

    namectr = {}

    def sb(name, shape, dtype, stack=es):
        n2 = namectr.get(name, 0)
        namectr[name] = n2 + 1
        uname = name if n2 == 0 else f"{name}__{n2}"
        h = stack.enter_context(nc.sbuf_tensor(uname, list(shape), dtype))
        t[name] = h
        return h

    sb("x_res", (128, NT * D), F32)
    sb("xnT", (128, IC * N), BF16)
    sb("dump", (128, 1024), F32)
    sb("ident", (128, 128), BF16)
    sb("tril", (128, 128), BF16)
    sb("ones", (1, 128), F32)
    sb("ones_bf", (1, 128), BF16)
    sb("ridx", (128, 1), I32)
    t["psbig"] = es.enter_context(nc.psum_tensor("psbig", [128, 3072], F32))
    t["ps"] = [t["psbig"][:, i * 512:(i + 1) * 512] for i in range(6)]
    t["pt"] = [es.enter_context(nc.psum_tensor(f"pt{i}", [128, 1024], BF16))
               for i in range(2)]

    p.dma("Y", t["x_res"][:], chunked_src(t["x"], 0, D, D, NT))
    p.dma("P", t["ident"][:], t["c_ident"])
    p.dma("P", t["tril"][:], t["c_tril"])
    p.dma("Y", t["ones"][:], t["c_ones"])
    p.dma("P", t["ones_bf"][:], t["c_ones"])
    p.dma("Y", t["ridx"][:], t["c_ridx"].rearrange("(p o) -> p o", o=1))
    p.barrier()

    for layer in (0, 1):
        lnst = "ln0" if layer == 0 else "ln1"
        if lnst in stages:
            with ExitStack() as lnes:
                sb("lnscratch", (128, 16), F32, lnes)
                sb("ln_xtile", (128, D), BF16, lnes)
                emit_ln(p, t, None, layer)
            if layer == 0 and "xnT" in dbg_out:
                p.dma("Y", dbg_out["xnT"], t["xnT"][:])
                p.barrier()
        atst = "attn0" if layer == 0 else "attn1"
        if atst in stages:
            with ExitStack() as pes:
                sb("qrot", (128, 2 * N), BF16, pes)
                sb("krot", (128, 2 * N), BF16, pes)
                sb("vg", (128, NT * 260), BF16, pes)
                sb("att", (128, IC * N), BF16, pes)
                sb("pbuf0", (128, 1024), BF16, pes)
                sb("pbuf1", (128, 1024), BF16, pes)
                sb("rr", (1, N), F32, pes)
                if layer == 1:
                    sb("bq8", (128, 8), F32, pes)
                    sb("bkt", (128, 8), F32, pes)
                    sb("bvrow", (1, D), BF16, pes)
                    sb("borow", (1, D), BF16, pes)
                    sb("relb", (1, 496), F32, pes)
                    sb("relb_rev", (1, 496), F32, pes)
                    sb("frrow", (1, 4095), BF16, pes)
                    sb("strip", (128, 3968), BF16, pes)
                else:
                    t["bvrow"] = None
                with ExitStack() as pes2:
                    sb("wg", (128, IC * 256), BF16, pes2)
                    if layer == 0:
                        sb("qe", (128, N), BF16, pes2)
                        sb("qo", (128, N), BF16, pes2)
                        sb("tmpb", (128, N), BF16, pes2)
                        sb("cos", (128, N), BF16, pes2)
                        sb("sin", (128, N), BF16, pes2)
                        p.dma("P", t["cos"][:], t["c_cos"])
                        p.dma("P", t["sin"][:], t["c_sin"])
                    emit_attn_core(p, t, None, layer)
                if layer == 1 and "att1" in dbg_out:
                    p.barrier("Y")
                    p.dma("Y", dbg_out["att1"], t["att"][:])
                    p.dma("Y", dbg_out["strip1"], t["strip"][:])
                    p.barrier()
                sb("wo_sb", (128, IC * 512), BF16, pes)
                emit_attn_proj(p, t, None, layer)
        if layer == 0 and "ffn0" in stages:
            with ExitStack() as pes:
                sb("gate", (128, 128), F32, pes)
                sb("idxI", (128, 128), I32, pes)
                sb("lin1", (128, 1), F32, pes)
                sb("lin2", (128, 1), F32, pes)
                sb("lg1", (128, 1), F32, pes)
                with ExitStack() as pes2:
                    sb("mxA", (128, 128), F32, pes2)
                    sb("mxB", (128, 128), F32, pes2)
                    sb("ixA", (128, 128), F32, pes2)
                    sb("ixB", (128, 128), F32, pes2)
                    sb("am", (128, 8), F32, pes2)
                    sb("ar", (128, 1), F32, pes2)
                    sb("keys_sb", (128, 8192), BF16, pes2)
                    t["wqf"] = [pes2.enter_context(
                        nc.sbuf_tensor(f"wqf{i}", [128, IC * 512], BF16))
                        for i in range(2)]
                    sb("qph", (128, 4 * N), BF16, pes2)
                    sb("sim_sb", (128, 128), F32, pes2)
                    sb("riota", (128, 128), F32, pes2)
                    t["riota_tok"] = p.dma("Y", t["riota"][:], t["c_riota"])
                    emit_ffn0_sel(p, t, None)
                sb("win", (128, IC * DI), BF16, pes)
                sb("wout", (128, 16 * D), BF16, pes)
                sb("gbuf0", (128, 3072), BF16, pes)
                sb("gbuf1", (128, 3072), BF16, pes)
                sb("hid0", (128, DI), F32, pes)
                sb("hid1", (128, DI), F32, pes)
                sb("hid_bf", (128, DI), BF16, pes)
                sb("hidT", (128, DI), BF16, pes)
                sb("xn_tile", (128, D), BF16, pes)
                emit_ffn0_main(p, t, None)
        if layer == 1 and "moe" in stages:
            with ExitStack() as pes:
                sb("gw_sb", (128, IC * 16), BF16, pes)
                sb("ebrow", (1, NE), F32, pes)
                sb("ebb", (128, NE), F32, pes)
                sb("scores", (128, NT * 16), F32, pes)
                sb("comb", (128, NT * 16), F32, pes)
                sb("m8r", (128, 8), F32, pes)
                sb("mr16", (128, 16), F32, pes)
                sb("gsc", (128, 8), F32, pes)
                sb("t16", (128, 16), F32, pes)
                sb("t16b", (128, 16), F32, pes)
                sb("gm", (128, 4), F32, pes)
                sb("pr4", (128, 4), F32, pes)
                sb("smask", (128, 16), F32, pes)
                sb("wsel", (128, 16), F32, pes)
                sb("dn", (128, 1), F32, pes)
                t["gp_sb"] = [pes.enter_context(
                    nc.sbuf_tensor(f"gp_sb{i}", [128, IC * 512], BF16))
                    for i in range(2)]
                t["up_sb"] = [pes.enter_context(
                    nc.sbuf_tensor(f"up_sb{i}", [128, IC * 512], BF16))
                    for i in range(2)]
                t["dp_sb"] = [pes.enter_context(
                    nc.sbuf_tensor(f"dp_sb{i}", [128, 4 * D], BF16))
                    for i in range(2)]
                sb("act", (128, 4 * N), BF16, pes)
                sb("stmp", (128, 1024), F32, pes)
                emit_moe(p, t, None)

    p.barrier("Y")
    p.dma("Y", bass.AP(out.tensor, 0, [[D, 128], [128 * D, NT], [1, D]]),
          t["x_res"][:], sem="dO")
    p.cmds["Y"].append(lambda E, h=p.sem["dO"], v=p.cnt["dO"]: E.wait_ge(h, v))

    with nc.Block() as block:
        p.finalize(block)
    es.close()
    return nc


_CACHED = {}


def kernel(**inputs):
    x_full = inputs["L0_attn_x"]
    prep = host_prep(inputs)
    nc = build_nc()
    in_maps = []
    for b in range(8):
        m = dict(prep)
        m["x"] = np.ascontiguousarray(x_full[b])
        in_maps.append(m)
    res = run_bass_kernel_spmd(nc, in_maps, core_ids=list(range(8)))
    out = np.stack([res.results[b]["out"] for b in range(8)], axis=0)
    return out.astype(np.float32)


# version 07 (final)
# speedup vs baseline: 1.0953x; 1.0953x over previous
import numpy as np
import ml_dtypes
import concourse.bass as bass
import concourse.mybir as mybir
from concourse.bass_utils import run_bass_kernel_spmd
from contextlib import ExitStack

F32 = mybir.dt.float32
BF16 = mybir.dt.bfloat16
I32 = mybir.dt.int32
AF = mybir.ActivationFunctionType
ALU = mybir.AluOpType
AX = mybir.AxisListType
BF = ml_dtypes.bfloat16

N = 2048
D = 1024
NT = 16
IC = 8
NH = 16
DK = 64
PKH = 8
DIMKEY = 512
DI = 2048
NE = 16
EI = 512

ALL_STAGES = ("ln0", "attn0", "ffn0", "ln1", "attn1", "moe")


class Prog:

    ENG = {"T": "tensor", "V": "vector", "S": "scalar", "P": "gpsimd", "Y": "sync"}
    ESEM = {"T": "sT", "V": "sV", "S": "sS", "P": "sP"}
    DSEM = {"Y": "dY", "P": "dP", "S": "dS"}

    def __init__(self, nc, es):
        self.nc = nc
        self.cmds = {e: [] for e in self.ENG}
        self.sem = {}
        for name in ("sT", "sV", "sS", "sP", "dY", "dP", "dS", "dO"):
            self.sem[name] = es.enter_context(nc.semaphore(name))
        self.cnt = {k: 0 for k in self.sem}
        self.lastw = {e: {} for e in self.ENG}

    def wait(self, eng, tok):
        if tok is None:
            return
        sem, val = tok
        if val <= 0 or self.lastw[eng].get(sem, 0) >= val:
            return
        self.lastw[eng][sem] = val
        h = self.sem[sem]
        self.cmds[eng].append(lambda E, h=h, v=val: E.wait_ge(h, v))

    def waits(self, eng, toks):
        for t in toks:
            self.wait(eng, t)

    def op(self, eng, fn, waits=()):
        for t in waits:
            self.wait(eng, t)
        sem = self.ESEM[eng]
        h = self.sem[sem]
        self.cnt[sem] += 1
        v = self.cnt[sem]
        self.cmds[eng].append(lambda E, fn=fn, h=h: fn(E).then_inc(h, 1))
        return (sem, v)

    def dma(self, eng, out, in_, waits=(), sem=None, slow=False):
        for t in waits:
            self.wait(eng, t)
        sem = sem or self.DSEM[eng]
        h = self.sem[sem]
        self.cnt[sem] += 16
        v = self.cnt[sem]
        self.cmds[eng].append(
            lambda E, o=out, i=in_, h=h, s=slow: E.dma_start(
                out=o, in_=i, allow_slow_non_contiguous=s).then_inc(h, 16))
        return (sem, v)

    def gather(self, out, table, idx_ap, waits=()):
        for t in waits:
            self.wait("P", t)
        h = self.sem["dP"]
        self.cnt["dP"] += 16
        v = self.cnt["dP"]
        self.cmds["P"].append(
            lambda E, o=out, tb=table, ix=idx_ap, h=h: E.indirect_dma_start(
                out=o, out_offset=None, in_=tb,
                in_offset=bass.IndirectOffsetOnAxis(ap=ix, axis=0),
            ).then_inc(h, 16))
        return ("dP", v)

    def now(self):
        return [(s, self.cnt[s]) for s in
                ("sT", "sV", "sS", "sP", "dY", "dP", "dS") if self.cnt[s] > 0]

    def barrier(self, engines="TVSPY"):
        toks = self.now()
        for e in engines:
            self.waits(e, toks)

    def finalize(self, block):
        for e, attr in self.ENG.items():
            cmds = self.cmds[e]
            if not cmds:
                continue
            getattr(block, attr)(lambda E, cmds=cmds: [c(E) for c in cmds])


def rope_perm():
    perm = []
    for g in range(4):
        for h in range(4 * g, 4 * g + 4):
            perm.extend(h * DK + 2 * i for i in range(DK // 2))
        for h in range(4 * g, 4 * g + 4):
            perm.extend(h * DK + 2 * i + 1 for i in range(DK // 2))
    return np.array(perm)


def host_prep(inp):
    d = {}
    perm = rope_perm()
    d["l0_wqT"] = np.ascontiguousarray(inp["L0_attn_Wq"][perm].T)
    d["l0_wkT"] = np.ascontiguousarray(inp["L0_attn_Wk"][perm].T)
    d["l0_wvT"] = np.ascontiguousarray(inp["L0_attn_Wv"].T)
    d["l0_woT"] = np.ascontiguousarray(inp["L0_attn_Wo"].T)
    d["l0f_wqT"] = np.ascontiguousarray(inp["L0_ffn_Wq"].T)
    kt = np.ascontiguousarray(np.transpose(inp["L0_ffn_keys"], (2, 0, 3, 1)))
    d["l0f_keysT"] = np.ascontiguousarray(kt.reshape(2 * 8, 4, 128, 128))
    d["l0f_winT"] = np.ascontiguousarray(inp["L0_ffn_W_in"].T)
    d["l0f_woutT"] = np.ascontiguousarray(inp["L0_ffn_W_out"].T)
    d["t_in"] = np.ascontiguousarray(
        np.concatenate([inp["L0_ffn_in_a"], inp["L0_ffn_in_b"]], axis=1))
    d["t_out"] = np.ascontiguousarray(
        np.concatenate([inp["L0_ffn_out_a"], inp["L0_ffn_out_b"]], axis=1))
    d["l1_wqT"] = np.ascontiguousarray(inp["L1_attn_Wq"].T)
    d["l1_wkT"] = np.ascontiguousarray(inp["L1_attn_Wk"].T)
    d["l1_wvT"] = np.ascontiguousarray(inp["L1_attn_Wv"].T)
    d["l1_woT"] = np.ascontiguousarray(inp["L1_attn_Wo"].T)
    d["l1_bq"] = np.ascontiguousarray(inp["L1_attn_bq"])
    d["l1_bk"] = np.ascontiguousarray(inp["L1_attn_bk"])
    d["l1_bv"] = np.ascontiguousarray(inp["L1_attn_bv"]).reshape(1, D)
    d["l1_bo"] = np.ascontiguousarray(inp["L1_attn_bo"]).reshape(1, D)
    d["l1_relb"] = np.ascontiguousarray(inp["L1_attn_rel_bias"].T).reshape(1, 16 * 31)
    d["l1_relb_rev"] = np.ascontiguousarray(
        inp["L1_attn_rel_bias"][::-1].T).reshape(1, 16 * 31)
    d["l1_gwT"] = np.ascontiguousarray(inp["L1_ffn_gate_weight"].T)
    d["l1_ebias"] = np.ascontiguousarray(inp["L1_ffn_e_bias"]).reshape(1, NE)
    d["l1_gpT"] = np.ascontiguousarray(np.transpose(inp["L1_ffn_gate_proj"], (0, 2, 1)))
    d["l1_upT"] = np.ascontiguousarray(np.transpose(inp["L1_ffn_up_proj"], (0, 2, 1)))
    d["l1_dpT"] = np.ascontiguousarray(np.transpose(inp["L1_ffn_down_proj"], (0, 2, 1)))

    inv = 10000.0 ** (-np.arange(32, dtype=np.float64) * (2.0 / DK))
    ang = np.arange(N, dtype=np.float64)[None, :] * inv[:, None]
    d["c_cos"] = np.tile(np.cos(ang), (4, 1)).astype(BF)
    d["c_sin"] = np.tile(np.sin(ang), (4, 1)).astype(BF)
    d["c_ident"] = np.eye(128, dtype=BF)
    kk = np.arange(128)[:, None]
    qq = np.arange(128)[None, :]
    d["c_tril"] = (kk <= qq).astype(BF)
    d["c_riota"] = np.broadcast_to(
        127.0 - np.arange(128, dtype=np.float32), (128, 128)).copy()
    d["c_ones"] = np.ones((1, 128), dtype=np.float32)
    d["c_ridx"] = (127 - np.arange(128)).astype(np.int32)
    pm = np.zeros((4, 128, 128), dtype=BF)
    for tc in range(2):
        for hl2 in range(2):
            hl = 2 * tc + hl2
            for i in range(32):
                pm[tc * 2 + 0, hl * 32 + i, hl2 * 64 + i] = 1.0
                pm[tc * 2 + 1, hl * 32 + i, hl2 * 64 + 32 + i] = 1.0
    d["c_perm"] = pm
    return d


WEIGHT_SPECS = {
    "x": (N, D),
    "l0_wqT": (D, D), "l0_wkT": (D, D), "l0_wvT": (D, D), "l0_woT": (D, D),
    "l0f_wqT": (D, 8192), "l0f_keysT": (16, 4, 128, 128),
    "l0f_winT": (D, DI), "l0f_woutT": (DI, D),
    "t_in": (16384, 3072), "t_out": (16384, 3072),
    "l1_wqT": (D, D), "l1_wkT": (D, D), "l1_wvT": (D, D), "l1_woT": (D, D),
    "l1_bq": (D,), "l1_bk": (D,), "l1_bv": (1, D), "l1_bo": (1, D),
    "l1_relb": (1, 496), "l1_relb_rev": (1, 496),
    "l1_gwT": (D, NE), "l1_ebias": (1, NE),
    "l1_gpT": (NE, D, EI), "l1_upT": (NE, D, EI), "l1_dpT": (NE, EI, D),
    "c_riota": (128, 128), "c_ones": (1, 128), "c_ridx": (128,),
}
BF16_INPUTS = {"c_cos": (128, N), "c_sin": (128, N), "c_ident": (128, 128),
               "c_tril": (128, 128), "c_perm": (4, 128, 128)}



def chunked_src(ap, col0, full_w, w, nchunk, rowblk=128):
    return bass.AP(ap.tensor, ap.offset + col0,
                   [[full_w, rowblk], [rowblk * full_w, nchunk], [1, w]])

def emit_ln(p, t, c, layer):
    nc = p.nc
    x_res, xnT = t["x_res"], t["xnT"]
    ls = t["lnscratch"]
    xt = t["ln_xtile"][:]
    dump = t["dump"]
    ev_tok = None
    tp_toks = [None, None]
    for nt in range(NT):
        xs = x_res[:, nt * D:(nt + 1) * D]
        s_ = ls[:, 0:1]; ss = ls[:, 1:2]; mean = ls[:, 2:3]; m2 = ls[:, 3:4]
        ve = ls[:, 4:5]; sd = ls[:, 5:6]; rstd = ls[:, 6:7]; nmr = ls[:, 7:8]
        t1 = p.op("V", lambda V, xs=xs, o=s_: V.reduce_sum(o, xs, axis=AX.X))
        t2 = p.op("S", lambda S, xs=xs, o=ss, dmp=dump[:, 0:D]: S.activation(
            dmp, xs, AF.Square, accum_out=o))
        t3 = p.op("V", lambda V, o=mean, i=s_: V.tensor_scalar(
            o, i, 1.0 / D, None, ALU.mult), waits=[t1])
        t4 = p.op("V", lambda V, o=m2, i=mean: V.tensor_tensor(
            o, i, i, ALU.mult), waits=[t3])
        t5 = p.op("V", lambda V, o=ve, i=ss, j=m2: V.scalar_tensor_tensor(
            o, i, 1.0 / D, j, ALU.mult, ALU.subtract), waits=[t2, t4])
        t5b = p.op("V", lambda V, o=ve, i=ve: V.tensor_scalar(
            o, i, 1e-5, None, ALU.add), waits=[t5])
        t6 = p.op("S", lambda S, o=sd, i=ve: S.activation(
            o, i, AF.Sqrt), waits=[t5b])
        t7 = p.op("V", lambda V, o=rstd, i=sd: V.reciprocal(o, i), waits=[t6])
        t8 = p.op("V", lambda V, o=nmr, i=mean, j=rstd: V.scalar_tensor_tensor(
            o, i, -1.0, j, ALU.mult, ALU.mult), waits=[t7])
        t9 = p.op("S", lambda S, o=xt, xs=xs, sc=rstd, b=nmr: S.activation(
            o, xs, AF.Identity, scale=sc, bias=b), waits=[t8])
        bank = t["pt"][nt % 2]
        tts = []
        for ic2 in range(IC):
            w = [t9, tp_toks[nt % 2]] if ic2 == 0 else []
            tts.append(p.op("T", lambda T, i=xt[:, ic2 * 128:(ic2 + 1) * 128],
                            o=bank[:, ic2 * 128:(ic2 + 1) * 128],
                            idn=t["ident"][:]: T.transpose(o, i, idn), waits=w))
        out_ap = bass.AP(xnT, nt * 128, [[IC * N, 128], [N, IC], [1, 128]])
        in_ap = bank[:, 0:IC * 128]
        tp_toks[nt % 2] = p.op("V", lambda V, o=out_ap, i=in_ap: V.tensor_copy(o, i),
                               waits=[tts[-1]])
    p.barrier()


def emit_attn_core(p, t, c, layer):
    nc = p.nc
    l1 = layer == 1
    xnT = t["xnT"]
    wg = t["wg"]
    if not l1:
        qe, qo, tmpb = t["qe"], t["qo"], t["tmpb"]
    else:
        qe = qo = tmpb = None
    qrot, krot = t["qrot"], t["krot"]
    vg, att = t["vg"], t["att"]
    pbufs = [t["pbuf0"], t["pbuf1"]]
    unn = t["dump"]
    rr = t["rr"]
    ps = t["ps"]
    wname = "l1" if l1 else "l0"
    wq_h = t[f"{wname}_wqT"]; wk_h = t[f"{wname}_wkT"]
    wv_h = t[f"{wname}_wvT"]; wo_h = t[f"{wname}_woT"]

    bias_toks = []
    if l1:
        bq8, bkt = t["bq8"], t["bkt"]
        tb1 = p.dma("Y", bq8[:], t["l1_bq"].rearrange("(c p) -> p c", p=128), slow=True)
        tb2 = p.dma("Y", bkt[:], t["l1_bk"].rearrange("(c p) -> p c", p=128), slow=True)
        tb3 = p.dma("P", t["bvrow"][:], t["l1_bv"])
        tb4 = p.dma("P", t["borow"][:], t["l1_bo"])
        tb5 = p.op("V", lambda V, o=bq8[:], i=bq8[:]: V.tensor_scalar(
            o, i, 0.125, None, ALU.mult), waits=[tb1])
        bias_toks = [tb2, tb3, tb4, tb5]
        tb6 = p.dma("Y", t["relb"][:], t["l1_relb"])
        tb7 = p.dma("Y", t["relb_rev"][:], t["l1_relb_rev"])
        bias_toks += [tb6, tb7]

    use_tok = {}
    t["attn_use_tok"] = use_tok
    for g in range(4):
        osl = slice(g * 256, (g + 1) * 256)
        for which, wh in (("q", wq_h), ("k", wk_h)):
            wsb = wg
            lwt = p.dma("P", wsb[:], chunked_src(wh, osl.start, D, 256, IC),
                        waits=[use_tok.get("wg")])
            dst_e, dst_o = (qe, qo)
            rot = qrot if which == "q" else krot
            for oc2 in range(2):
                for pairq in range(2):
                    mmlast = []
                    for ncq in (2 * pairq, 2 * pairq + 1):
                        mms = []
                        for ic in range(IC):
                            first = ic == 0
                            w0 = [lwt, use_tok.get(("ps", ncq)),
                                  use_tok.get(("sps", ncq)),
                                  use_tok.get(("avps", ncq - 2))] if ic == 0 else []
                            mms.append(p.op("T", lambda T, o=ps[ncq][:, 0:512],
                                            l=wsb[:, ic * 256 + oc2 * 128: ic * 256 + (oc2 + 1) * 128],
                                            r=xnT[:, ic * N + ncq * 512: ic * N + (ncq + 1) * 512],
                                            st=first, sp=ic == IC - 1:
                                            T.matmul(o, l, r, start=st, stop=sp),
                                            waits=w0))
                        use_tok["wg"] = mms[-1]
                        mmlast.append(mms[-1])
                    dst = (dst_e if oc2 == 0 else dst_o) if not l1 else rot
                    dcol = pairq * 1024 if not l1 else (oc2 * N + pairq * 1024)
                    scale = 0.125 if which == "q" else 1.0
                    kw = {}
                    if l1:
                        bt = bq8 if which == "q" else bkt
                        kw["bias"] = bt[:, g * 2 + oc2: g * 2 + oc2 + 1]
                    ev = p.op("S", lambda S, o=dst[:, dcol:dcol + 1024],
                              i=t["psbig"][:, pairq * 1024:(pairq + 1) * 1024],
                              sc=scale, kw=kw:
                              S.activation(o, i, AF.Identity, scale=sc, **kw),
                              waits=mmlast + [use_tok.get("qeqo")]
                              + (bias_toks if l1 else []))
                    use_tok[("ps", 2 * pairq)] = ev
                    use_tok[("ps", 2 * pairq + 1)] = ev
            if not l1:
                evs = [use_tok[("ps", i)] for i in range(4)]
                m1 = p.op("V", lambda V, o=rot[:, 0:N], a=dst_e[:], cc=t["cos"][:]:
                          V.tensor_tensor(o, a, cc, ALU.mult),
                          waits=evs + [use_tok.get("rotbuf_" + which)])
                m2 = p.op("V", lambda V, o=tmpb[:], a=dst_o[:], cc=t["sin"][:]:
                          V.tensor_tensor(o, a, cc, ALU.mult))
                m3 = p.op("V", lambda V, o=rot[:, 0:N], a=rot[:, 0:N], b=tmpb[:]:
                          V.tensor_tensor(o, a, b, ALU.subtract), waits=[m1, m2])
                m4 = p.op("V", lambda V, o=rot[:, N:2 * N], a=dst_e[:], cc=t["sin"][:]:
                          V.tensor_tensor(o, a, cc, ALU.mult))
                m5 = p.op("V", lambda V, o=tmpb[:], a=dst_o[:], cc=t["cos"][:]:
                          V.tensor_tensor(o, a, cc, ALU.mult), waits=[m3])
                m6 = p.op("V", lambda V, o=rot[:, N:2 * N], a=rot[:, N:2 * N],
                          b=tmpb[:]: V.tensor_tensor(o, a, b, ALU.add),
                          waits=[m4, m5])
                use_tok[id(rot)] = m6
                use_tok["qeqo"] = m6
                rot2 = t["qrot2"] if which == "q" else t["krot2"]
                pm = t["pm_sb"]
                for tc in range(2):
                    pmms = []
                    for qc4 in range(4):
                        pmms.append(p.op("T", lambda T,
                            o=t["psbig"][:, qc4 * 512:(qc4 + 1) * 512],
                            l=pm[:, (tc * 2) * 128:(tc * 2 + 1) * 128],
                            r=rot[:, qc4 * 512:(qc4 + 1) * 512]:
                            T.matmul(o, l, r, start=True, stop=False),
                            waits=[m6, t["pm_tok"], use_tok.get(("ps", qc4)),
                                   use_tok.get(("sps", qc4)),
                                   use_tok.get(("avps", qc4 - 2))]))
                        pmms.append(p.op("T", lambda T,
                            o=t["psbig"][:, qc4 * 512:(qc4 + 1) * 512],
                            l=pm[:, (tc * 2 + 1) * 128:(tc * 2 + 2) * 128],
                            r=rot[:, N + qc4 * 512: N + (qc4 + 1) * 512]:
                            T.matmul(o, l, r, start=False, stop=True)))
                    pev = p.op("S", lambda S, o=rot2[:, tc * N:(tc + 1) * N],
                               i=t["psbig"][:, 0:N]:
                               S.activation(o, i, AF.Copy),
                               waits=pmms + [use_tok.get("rot2_" + which)])
                    for qc4 in range(4):
                        use_tok[("ps", qc4)] = pev
                        use_tok[("sps", qc4)] = pev
                use_tok[id(rot2)] = pev
            else:
                use_tok[id(rot)] = use_tok[("ps", 3)]
        lwv = p.dma("P", wg[:], chunked_src(wv_h, osl.start, D, 256, IC),
                    waits=[use_tok.get("wg")])
        vz = p.op("V", lambda V, o=bass.AP(vg, 64, [[NT * 260, 128], [65, NT * 4]]):
                  V.memset(o, 1.0), waits=[use_tok.get("vg_done")])
        v_toks = []
        for nt in range(NT):
            mms = []
            for ic in range(IC):
                w0 = [lwv, use_tok.get(("ps", 4)), use_tok.get(("sps", 4)),
                      use_tok.get(("sps", 5)), use_tok.get(("rbps", 0)),
                      use_tok.get(("rbps", 1))] if ic == 0 else []
                mms.append(p.op("T", lambda T, o=ps[4][:, 0:256],
                                l=xnT[:, ic * N + nt * 128: ic * N + (nt + 1) * 128],
                                r=wg[:, ic * 256:(ic + 1) * 256],
                                st=ic == 0, sp=(ic == IC - 1) and not l1:
                                T.matmul(o, l, r, start=st, stop=sp),
                                waits=w0))
            use_tok["wg"] = mms[-1]
            if l1:
                mms.append(p.op("T", lambda T, o=ps[4][:, 0:256],
                                l=t["ones_bf"][:],
                                r=t["bvrow"][:, osl]:
                                T.matmul(o, l, r, start=False, stop=True)))
            out_ap = bass.AP(vg, nt * 260, [[NT * 260, 128], [65, 4], [1, 64]])
            in_ap = bass.AP(t["psbig"], 4 * 512, [[3072, 128], [64, 4], [1, 64]])
            ev = p.op("S", lambda S, o=out_ap, i=in_ap: S.activation(
                o, i, AF.Copy), waits=[mms[-1], vz])
            use_tok[("ps", 4)] = ev
            v_toks.append(ev)
        for hh in range(4):
            h = 4 * g + hh
            cidx = hh // 2
            base = (hh % 2) * 64
            if not l1:
                ksc, qsc = t["krot2"], t["qrot2"]
            else:
                ksc, qsc = krot, qrot
            if l1:
                def build_strip(hh2):
                    par = hh2 % 2
                    frrow = t["frrow"][:]
                    f1 = p.op("V", lambda V, o=frrow[:, 0:2033],
                              b=t["relb"][:, hh2 * 31 + 30: hh2 * 31 + 31]:
                              V.tensor_copy(o, b.to_broadcast([1, 2033])),
                              waits=[use_tok.get("frrow")] + bias_toks)
                    f2 = p.op("V", lambda V, o=frrow[:, 2063:4095],
                              b=t["relb"][:, hh2 * 31: hh2 * 31 + 1]:
                              V.tensor_copy(o, b.to_broadcast([1, 2032])))
                    f3 = p.op("V", lambda V, o=frrow[:, 2033:2063],
                              i=t["relb_rev"][:, hh2 * 31 + 1: hh2 * 31 + 31]:
                              V.tensor_copy(o, i))
                    f4 = p.op("S", lambda S, o=frrow, i=frrow:
                              S.activation(o, i, AF.Exp), waits=[f1, f2, f3])
                    f5 = p.dma("Y", t["hbm_rows"][par].rearrange("(a b) -> a b", a=1),
                               frrow, waits=[f4, use_tok.get(("hbmrow", par))])
                    use_tok["frrow"] = f5
                    stok = p.gather(t["strips"][par][:],
                                    t["hbm_rows"][par].rearrange("(a b) -> a b", b=1),
                                    t["ridx"][:],
                                    waits=[f5, use_tok.get("attdone")])
                    use_tok[("hbmrow", par)] = stok
                    use_tok[("strip_tok", par)] = stok
                if h == 0:
                    build_strip(0)
                if h < NH - 1:
                    build_strip(h + 1)
                st_tok = use_tok[("strip_tok", h % 2)]
                strip_cur = t["strips"][h % 2]
                lncs = []
                for ci, rc in enumerate((0, 30)):
                    lm = p.op("T", lambda T, o=t["psbig"][:, ci:ci + 1],
                              l=t["ones"][:],
                              r=t["relb"][:, h * 31 + rc: h * 31 + rc + 1]:
                              T.matmul(o, l, r, start=True, stop=True),
                              waits=[use_tok.get(("sps", 0))] + bias_toks)
                    lncs.append(p.op("S", lambda S, o=t["lnc"][:, ci:ci + 1],
                                     i=t["psbig"][:, ci:ci + 1]:
                                     S.activation(o, i, AF.Copy),
                                     waits=[lm, use_tok.get("lnc_free")]))
                use_tok[("sps", 0)] = lncs[-1]
                use_tok[("sps", 1)] = lncs[-1]
            for qh in range(2):
                ktmax = (qh + 1) * 8 if not l1 else NT
                av_toks = []
                expm_tok = {}
                smm_tok = {}
                avk_tok = {}

                def emit_S(kt):
                    pa = (0, 1) if kt % 2 == 0 else (4, 5)
                    smms = []
                    for qq in range(2):
                        w0 = [use_tok.get(("sps", pa[qq])),
                              use_tok.get(("ps", 4)) if pa[qq] >= 4 else None,
                              use_tok.get(("rbps", qq)) if pa[qq] >= 4 else None,
                              use_tok.get(id(qsc)), use_tok.get(id(ksc))]
                        smms.append(p.op("T", lambda T,
                            o=ps[pa[qq]][:, 0:512],
                            l=ksc[base:base + 64, cidx * N + kt * 128: cidx * N + (kt + 1) * 128],
                            r=qsc[base:base + 64, cidx * N + qh * 1024 + qq * 512: cidx * N + qh * 1024 + (qq + 1) * 512],
                            tp=(base, 0):
                            T.matmul(o, l, r, start=True, stop=True,
                                     tile_position=tp), waits=w0))
                    smm_tok[kt] = smms[-1]

                def emit_exp(kt):
                    pa = (0, 1) if kt % 2 == 0 else (4, 5)
                    bcol = pa[0] * 512
                    pb = pbufs[kt % 2]
                    diag = (kt * 128 - qh * 1024) if not l1 else -(10 ** 9)
                    lo = max(diag, 0)
                    pfree = avk_tok.get(kt - 2, use_tok.get(("pbuf", kt % 2)))
                    ets = []
                    kw = {}
                    const_tile = False
                    if l1:
                        if kt * 128 >= qh * 1024 + 1023 + 15:
                            kw["bias"] = t["lnc"][:, 1:2]
                            const_tile = True
                        elif kt * 128 + 127 + 15 <= qh * 1024:
                            kw["bias"] = t["lnc"][:, 0:1]
                            const_tile = True
                    if lo > 0:
                        ets.append(p.op("V", lambda V, o=pb[:, 0:lo]:
                                        V.memset(o, 0.0), waits=[pfree]))
                    ets.append(p.op("S", lambda S, o=pb[:, lo:1024],
                                    i=t["psbig"][:, bcol + lo:bcol + 1024],
                                    kw=kw:
                                    S.activation(o, i, AF.Exp, **kw),
                                    waits=[smm_tok[kt], pfree]
                                    + (lncs if l1 else [])))
                    if not l1 and 0 <= diag < 1024:
                        ets.append(p.op("V", lambda V, o=pb[:, lo:lo + 128],
                                        m=t["tril"][:]: V.tensor_tensor(
                                            o, o, m, ALU.mult), waits=[ets[-1]]))
                    if l1 and not const_tile:
                        soff = qh * 1024 + 1920 - kt * 128
                        ets.append(p.op("V", lambda E, o=pb[:],
                                        m=strip_cur[:, soff:soff + 1024]:
                                        E.tensor_tensor(o, o, m, ALU.mult),
                                        waits=ets + [st_tok]))
                    use_tok[("sps", pa[0])] = ets[-1]
                    use_tok[("sps", pa[1])] = ets[-1]
                    expm_tok[kt] = ets[-1]

                def emit_AV(kt):
                    pb = pbufs[kt % 2]
                    for qq in range(2):
                        av_toks.append(p.op("T", lambda T,
                            o=ps[2 + qq][0:65, 0:512],
                            l=vg[:, kt * 260 + hh * 65: kt * 260 + (hh + 1) * 65],
                            r=pb[:, qq * 512:(qq + 1) * 512],
                            st=kt == 0, sp=kt == ktmax - 1:
                            T.matmul(o, l, r, start=st, stop=sp),
                            waits=[expm_tok[kt], v_toks[-1],
                                   use_tok.get(("avps", qq))]))
                    avk_tok[kt] = av_toks[-1]

                for kt in range(ktmax):
                    emit_S(kt)
                    if kt >= 1:
                        emit_exp(kt - 1)
                        emit_AV(kt - 1)
                emit_exp(ktmax - 1)
                emit_AV(ktmax - 1)
                use_tok[("pbuf", 0)] = av_toks[-1]
                use_tok[("pbuf", 1)] = av_toks[-1]
                d1 = p.op("V", lambda V, o=rr[:, 0:512], i=ps[2][64:65, 0:512]:
                          V.tensor_copy(o, i),
                          waits=[av_toks[-1], av_toks[-2], use_tok.get("rr")])
                d2 = p.op("V", lambda V, o=rr[:, 512:1024], i=ps[3][64:65, 0:512]:
                          V.tensor_copy(o, i))
                d3a = p.op("S", lambda S, o=rr[:, 1024:2048], i=rr[:, 0:1024]:
                           S.activation(o, i, AF.Ln), waits=[d1, d2])
                d3 = p.op("S", lambda S, o=rr[:, 0:1024], i=rr[:, 1024:2048]:
                          S.activation(o, i, AF.Exp, scale=-1.0), waits=[d3a])
                rbs = []
                for qq in range(2):
                    rbs.append(p.op("T", lambda T, o=ps[4 + qq][0:64, 0:512],
                                    l=t["ones"][:, 0:64],
                                    r=rr[:, qq * 512:(qq + 1) * 512]:
                                    T.matmul(o, l, r, start=True, stop=True),
                                    waits=[d3, use_tok.get(("rbps", qq))]))
                use_tok["rr"] = rbs[-1]
                uevs = []
                for qq in range(2):
                    uevs.append(p.op("S", lambda S, o=unn[0:64, qq * 512:(qq + 1) * 512],
                                     i=ps[2 + qq][0:64, 0:512]:
                                     S.activation(o, i, AF.Copy),
                                     waits=[av_toks[-1], av_toks[-2], d1, d2,
                                            use_tok.get(("unn", qq))]))
                use_tok[("avps", 0)] = uevs[0]
                use_tok[("avps", 1)] = uevs[1]
                acol = (h // 2) * N + qh * 1024
                arow = (h % 2) * 64
                nrm = []
                for qq in range(2):
                    nrm.append(p.op("V", lambda V,
                        o=att[arow:arow + 64, acol + qq * 512: acol + (qq + 1) * 512],
                        a=unn[0:64, qq * 512:(qq + 1) * 512],
                        b=ps[4 + qq][0:64, 0:512]:
                        V.tensor_tensor(o, a, b, ALU.mult),
                        waits=[uevs[qq], rbs[qq]]))
                use_tok[("rbps", 0)] = nrm[0]
                use_tok[("rbps", 1)] = nrm[1]
                use_tok[("unn", 0)] = nrm[0]
                use_tok[("unn", 1)] = nrm[1]
                use_tok[("attdone")] = nrm[-1]
        use_tok["vg_done"] = use_tok.get("attdone")
        use_tok["qeqo"] = use_tok.get("attdone")
        use_tok["rot2_q"] = use_tok.get("attdone")
        use_tok["rot2_k"] = use_tok.get("attdone")
        use_tok["lnc_free"] = use_tok.get("attdone")
        use_tok["rotbuf_q"] = use_tok.get("attdone")
        use_tok["rotbuf_k"] = use_tok.get("attdone")
        use_tok["wg"] = use_tok.get("attdone")


def emit_attn_proj(p, t, c, layer):
    l1 = layer == 1
    ps = t["ps"]
    att = t["att"]
    use_tok = t["attn_use_tok"]
    wo_h = t["l1_woT"] if l1 else t["l0_woT"]
    NTL = NT
    wo_sb = t["wo_sb"]
    att_done = use_tok.get("attdone")
    for oc in range(4):
        pj = oc % 2
        lwo = p.dma("P", wo_sb[:], chunked_src(wo_h, oc * 256, D, 256, IC),
                    waits=[att_done, use_tok.get("wo_free")])
        for nt in range(NT):
            mms = []
            if l1:
                mms.append(p.op("T", lambda T, o=ps[pj][:, 0:256],
                                l=t["ones_bf"][:],
                                r=t["borow"][:, oc * 256:(oc + 1) * 256]:
                                T.matmul(o, l, r, start=True, stop=False),
                                waits=[lwo, att_done, use_tok.get(("ops", pj))]))
            for ic in range(IC):
                first = (ic == 0) and not l1
                mms.append(p.op("T", lambda T, o=ps[pj][:, 0:256],
                                l=att[:, ic * N + nt * 128:ic * N + (nt + 1) * 128],
                                r=wo_sb[:, ic * 256:(ic + 1) * 256],
                                st=first, sp=ic == IC - 1:
                                T.matmul(o, l, r, start=st, stop=sp),
                                waits=([lwo, att_done, use_tok.get(("ops", pj))]
                                       if ic == 0 else [])))
            xa = t["x_res"][:, nt * D + oc * 256: nt * D + (oc + 1) * 256]
            ev = p.op("V", lambda V, o=xa, a=ps[pj][:, 0:256], b=xa:
                      V.tensor_tensor(o, a, b, ALU.add), waits=[mms[-1]])
            use_tok[("ops", pj)] = ev
            use_tok["wo_free"] = mms[-1]
    p.barrier()


def emit_ffn0_sel(p, t, c):
    nc = p.nc
    xnT = t["xnT"]
    ps = t["ps"]
    dump = t["dump"]
    keys_sb, wqf, qph = t["keys_sb"], t["wqf"], t["qph"]
    sim_sb, am, ar = t["sim_sb"][:], t["am"][:], t["ar"][:]
    mxA, mxB, ixA, ixB = t["mxA"][:], t["mxB"][:], t["ixA"][:], t["ixB"][:]
    gate, idxI = t["gate"][:], t["idxI"][:]
    lk = p.dma("P", keys_sb[:], bass.AP(
        t["l0f_keysT"].tensor, 0,
        [[128, 128], [4 * 128 * 128, 16], [128 * 128, 4], [1, 128]]))
    use = {}
    for ph in range(16):
        pp, h = divmod(ph, 8)
        wb = wqf[ph % 2]
        lwt = p.dma("P", wb[:], chunked_src(t["l0f_wqT"], ph * 512, 8192, 512, IC),
                    waits=[use.get(("wqf", ph % 2))])
        qlast = []
        for oc in range(4):
            for pairq in range(2):
                mml = []
                for ncq in (2 * pairq, 2 * pairq + 1):
                    mms = []
                    for ic in range(IC):
                        mms.append(p.op("T", lambda T, o=ps[ncq][:, 0:512],
                                        l=wb[:, ic * 512 + oc * 128: ic * 512 + (oc + 1) * 128],
                                        r=xnT[:, ic * N + ncq * 512:ic * N + (ncq + 1) * 512],
                                        st=ic == 0, sp=ic == IC - 1:
                                        T.matmul(o, l, r, start=st, stop=sp),
                                        waits=([lwt, use.get(("qps", pairq))]
                                               if ic == 0 else [])))
                    mml.append(mms[-1])
                ev = p.op("S", lambda S,
                    o=qph[:, oc * N + pairq * 1024: oc * N + (pairq + 1) * 1024],
                    i=t["psbig"][:, pairq * 1024:(pairq + 1) * 1024]:
                    S.activation(o, i, AF.Copy), waits=mml)
                use[("qps", pairq)] = ev
                qlast.append(ev)
        use[("wqf", ph % 2)] = qlast[-1]
        qph_done = qlast[-2:]
        mxX = mxA if pp == 0 else mxB
        ixX = ixA if pp == 0 else ixB
        for nt in range(NT):
            mms = []
            for dc in range(4):
                mms.append(p.op("T", lambda T, o=ps[4][:, 0:128],
                                l=qph[:, dc * N + nt * 128:dc * N + (nt + 1) * 128],
                                r=keys_sb[:, ph * 512 + dc * 128: ph * 512 + (dc + 1) * 128],
                                st=dc == 0, sp=dc == 3:
                                T.matmul(o, l, r, start=st, stop=sp),
                                waits=(qph_done + [lk, use.get("simev")]
                                       if dc == 0 else [])))
            sev = p.op("S", lambda S, o=sim_sb, i=ps[4][:, 0:128]:
                       S.activation(o, i, AF.Copy), waits=[mms[-1]])
            use["simev"] = sev
            a1 = p.op("V", lambda V, o=am, i=sim_sb: V.max(o, i), waits=[sev])
            a2 = p.op("V", lambda V, o=dump[:, 0:128], i=sim_sb, s=am[:, 0:1],
                      r=t["riota"][:]: V.scalar_tensor_tensor(
                          o, i, s, r, ALU.is_equal, ALU.mult),
                      waits=[a1, t["riota_tok"]])
            a3 = p.op("V", lambda V, o=ar, i=dump[:, 0:128]:
                      V.reduce_max(o, i, axis=AX.X), waits=[a2])
            a4 = p.op("V", lambda V, o=ixX[:, nt * 8 + h:nt * 8 + h + 1], i=ar:
                      V.tensor_scalar(o, i, -1.0, 127.0, ALU.mult, ALU.add),
                      waits=[a3])
            a5 = p.op("V", lambda V, o=mxX[:, nt * 8 + h:nt * 8 + h + 1],
                      i=am[:, 0:1]: V.tensor_copy(o, i), waits=[a1])
            use["simev"] = (a2[0], a2[1])
            use["lastsel"] = a5
    i1 = p.op("V", lambda V, o=gate, a=ixA, b=ixB: V.scalar_tensor_tensor(
        o, a, 128.0, b, ALU.mult, ALU.add),
        waits=[use["lastsel"]])
    i2 = p.op("V", lambda V, o=idxI, i=gate: V.tensor_copy(o, i), waits=[i1])
    i3 = p.op("V", lambda V, o=gate, a=mxA, b=mxB: V.tensor_tensor(
        o, a, b, ALU.add), waits=[i2])
    i4 = p.op("V", lambda V, o=gate, i=gate: V.tensor_scalar(
        o, i, 0.0, None, ALU.max), waits=[i3])
    t["sel_done"] = [i2, i4]
    p.barrier()


def emit_ffn0_main(p, t, c):
    xnT = t["xnT"]
    ps = t["ps"]
    dump = t["dump"]
    gate, idxI = t["gate"][:], t["idxI"][:]
    sel_done = t["sel_done"]
    win, wout = t["win"], t["wout"]
    gbufs = [t["gbuf0"], t["gbuf1"]]
    hids = [t["hid0"], t["hid1"]]
    hid_bf, hidT, xn_tile = t["hid_bf"], t["hidT"], t["xn_tile"]
    lin1, lg1 = t["lin1"][:], t["lg1"][:]
    lin2 = t["lin2"][:]
    lw1 = p.dma("P", win[:], chunked_src(t["l0f_winT"], 0, DI, DI, IC))
    lw2 = p.dma("P", wout[:], chunked_src(t["l0f_woutT"], 0, D, D, 16))
    use = {}
    st_nt = {}

    def emit_XT(nt):
        bank = t["pt"][0]
        tts = []
        for ic in range(IC):
            tts.append(p.op("T", lambda T, i=xnT[:, ic * N + nt * 128:ic * N + (nt + 1) * 128],
                            o=bank[:, ic * 128:(ic + 1) * 128], idn=t["ident"][:]:
                            T.transpose(o, i, idn),
                            waits=[use.get("pt0")] if ic == 0 else []))
        xt_ev = p.op("V", lambda V, o=xn_tile[:], i=bank[:, 0:D]:
                     V.tensor_copy(o, i),
                     waits=[tts[-1], use.get("xn_tile_free")])
        use["pt0"] = xt_ev
        st_nt[nt] = {"xt": xt_ev}

    def emit_WIN(nt):
        par = nt % 2
        hid = hids[par]
        hevs = []
        for pairq in range(2):
            mml = []
            for j in (2 * pairq, 2 * pairq + 1):
                mms = []
                for ic in range(IC):
                    mms.append(p.op("T", lambda T, o=ps[j][:, 0:512],
                                    l=xnT[:, ic * N + nt * 128:ic * N + (nt + 1) * 128],
                                    r=win[:, ic * DI + j * 512: ic * DI + (j + 1) * 512],
                                    st=ic == 0, sp=ic == IC - 1:
                                    T.matmul(o, l, r, start=st, stop=sp),
                                    waits=([lw1, use.get(("hps", pairq))]
                                           if ic == 0 else [])))
                mml.append(mms[-1])
            hevs.append(p.op("S", lambda S,
                o=hid[:, pairq * 1024:(pairq + 1) * 1024],
                i=t["psbig"][:, pairq * 1024:(pairq + 1) * 1024]:
                S.activation(o, i, AF.Copy),
                waits=mml + [use.get(("hid_free", par))]))
            use[("hps", pairq)] = hevs[-1]
        st_nt[nt]["hevs"] = hevs

    def emit_gather(nt, h, stage):
        gi = nt * 16 + (0 if stage == "A" else 8) + h
        par = gi % 2
        tok = p.gather(gbufs[par][:],
                       t["t_in"] if stage == "A" else t["t_out"],
                       idxI[:, nt * 8 + h:nt * 8 + h + 1],
                       waits=sel_done + [use.get(("gbuf", par))])
        st_nt[nt][("g", stage, h)] = (tok, par)

    def emit_EXA(nt, h):
        s = st_nt[nt]
        gt, par = s[("g", "A", h)]
        gbuf = gbufs[par]
        r1 = p.op("V", lambda V, o=dump[:, 0:D], i=xn_tile[:],
                  g2=gbuf[:, 0:D], acc=lin1:
                  V.scalar_tensor_tensor(o, i, 1.0, g2, ALU.bypass,
                                         ALU.mult, accum_out=acc),
                  waits=[gt, s["xt"]])
        r2 = p.op("V", lambda V, o=lg1, a=lin1,
                  b=gate[:, nt * 8 + h:nt * 8 + h + 1]:
                  V.tensor_tensor(o, a, b, ALU.mult), waits=[r1])
        hid = hids[nt % 2]
        r3 = p.op("V", lambda V, o=hid[:], i=gbuf[:, D:3072], s2=lg1, x=hid[:]:
                  V.scalar_tensor_tensor(o, i, s2, x, ALU.mult, ALU.add),
                  waits=[r2, s.get("exa")] + s["hevs"])
        use[("gbuf", par)] = r3
        s["exa"] = r3
        if h == 7:
            s["xa_last_r1"] = r1

    def emit_GELU(nt):
        s = st_nt[nt]
        par = nt % 2
        gl = p.op("S", lambda S, o=hid_bf[:], i=hids[par][:]:
                  S.activation(o, i, AF.Gelu),
                  waits=[s["exa"], use.get("hidbf_free"), use.get("hT_last_tt")])
        s["gl"] = gl
        use[("hid_free", par)] = gl
        use["xn_tile_free"] = s["xa_last_r1"]

    def emit_TR(nt):
        s = st_nt[nt]
        tevs = []
        for half in range(2):
            bank = t["pt"][half]
            bkey = "pt0" if half == 0 else "pt1"
            tts = []
            for ic2 in range(IC):
                icg = half * 8 + ic2
                tts.append(p.op("T", lambda T,
                                i=hid_bf[:, icg * 128:(icg + 1) * 128],
                                o=bank[:, ic2 * 128:(ic2 + 1) * 128],
                                idn=t["ident"][:]: T.transpose(o, i, idn),
                                waits=[s["gl"], use.get(bkey)] if ic2 == 0 else []))
            tevs.append(p.op("V", lambda V, o=hidT[:, half * 1024:(half + 1) * 1024],
                             i=bank[:, 0:1024]: V.tensor_copy(o, i),
                             waits=[tts[-1], use.get("hidT_free")]))
            use[bkey] = tevs[-1]
        use["hT_last_tt"] = tts[-1]
        s["tevs"] = tevs

    def emit_WOUT(nt):
        s = st_nt[nt]
        wmms = []
        for oc in range(2):
            mms = []
            for ic in range(16):
                mms.append(p.op("T", lambda T, o=ps[4 + oc][:, 0:512],
                                l=hidT[:, ic * 128:(ic + 1) * 128],
                                r=wout[:, ic * D + oc * 512: ic * D + (oc + 1) * 512],
                                st=ic == 0, sp=ic == 15:
                                T.matmul(o, l, r, start=st, stop=sp),
                                waits=([lw2, s["tevs"][0], s["tevs"][1],
                                        use.get("wps")] if ic == 0 else [])))
            wmms.append(mms[-1])
        xa = t["x_res"][:, nt * D:(nt + 1) * D]
        wp = p.op("V", lambda V, o=xa, a=t["psbig"][:, 2048:3072],
                  b=xa: V.tensor_tensor(o, a, b, ALU.add), waits=wmms)
        use["wps"] = wp
        use["hidT_free"] = wmms[-1]
        s["wps"] = [wp]

    def emit_EXB(nt, h):
        s = st_nt[nt]
        gt, par = s[("g", "B", h)]
        gbuf = gbufs[par]
        r1a = p.op("V", lambda V, o=dump[:, 0:D], i=hid_bf[:, 0:D],
                   g2=gbuf[:, 0:D], acc=lin1:
                   V.scalar_tensor_tensor(o, i, 1.0, g2, ALU.bypass,
                                          ALU.mult, accum_out=acc),
                   waits=[gt, s["gl"]])
        r1 = p.op("V", lambda V, o=dump[:, 0:D], i=hid_bf[:, D:DI],
                  g2=gbuf[:, D:DI], acc=lin2:
                  V.scalar_tensor_tensor(o, i, 1.0, g2, ALU.bypass,
                                         ALU.mult, accum_out=acc),
                  waits=[r1a])
        r2a = p.op("V", lambda V, o=lg1, a=lin1, b=lin2:
                   V.tensor_tensor(o, a, b, ALU.add), waits=[r1])
        r2 = p.op("V", lambda V, o=lg1, a=lg1,
                  b=gate[:, nt * 8 + h:nt * 8 + h + 1]:
                  V.tensor_tensor(o, a, b, ALU.mult), waits=[r2a])
        xa = t["x_res"][:, nt * D:(nt + 1) * D]
        r3 = p.op("V", lambda V, o=xa, i=gbuf[:, DI:3072], s2=lg1, x=xa:
                  V.scalar_tensor_tensor(o, i, s2, x, ALU.mult, ALU.add),
                  waits=[r2, s.get("exb")] + s["wps"])
        s["exb"] = r3
        use[("gbuf", par)] = r3
        if h == 7:
            use["hidbf_free"] = r1

    for nt in range(NT):
        emit_XT(nt)
        emit_WIN(nt)
        if nt >= 1:
            emit_TR(nt - 1)
            emit_WOUT(nt - 1)
            for h in range(PKH):
                emit_gather(nt - 1, h, "B")
                emit_EXB(nt - 1, h)
        for h in range(PKH):
            emit_gather(nt, h, "A")
            emit_EXA(nt, h)
        emit_GELU(nt)
    emit_TR(NT - 1)
    emit_WOUT(NT - 1)
    for h in range(PKH):
        emit_gather(NT - 1, h, "B")
        emit_EXB(NT - 1, h)
    p.barrier()


def emit_moe(p, t, c):
    nc = p.nc
    xnT, ps, dump = t["xnT"], t["ps"], t["dump"]
    gw_sb, ebb = t["gw_sb"], t["ebb"][:]
    scores, comb = t["scores"], t["comb"]
    m8r, mr16, gsc = t["m8r"][:], t["mr16"][:], t["gsc"][:]
    t16, t16b, gm, smask, wsel = (t["t16"][:], t["t16b"][:], t["gm"][:],
                                  t["smask"][:], t["wsel"][:])
    dn = t["dn"][:]
    lg = p.dma("P", gw_sb[:], chunked_src(t["l1_gwT"], 0, NE, NE, IC))
    le = p.dma("Y", t["ebrow"][:], t["l1_ebias"])
    eb1 = p.op("T", lambda T, o=ps[5][:, 0:16], l=t["ones"][:], r=t["ebrow"][:]:
               T.matmul(o, l, r, start=True, stop=True), waits=[le])
    eb2 = p.op("V", lambda V, o=ebb[:], i=ps[5][:, 0:16]: V.tensor_copy(o, i),
               waits=[eb1])
    use = {}
    for nt in range(NT):
        mms = []
        for ic in range(IC):
            mms.append(p.op("T", lambda T, o=ps[5][:, 0:16],
                            l=xnT[:, ic * N + nt * 128:ic * N + (nt + 1) * 128],
                            r=gw_sb[:, ic * 16:(ic + 1) * 16],
                            st=ic == 0, sp=ic == IC - 1:
                            T.matmul(o, l, r, start=st, stop=sp),
                            waits=([lg, eb2, use.get("rps")] if ic == 0 else [])))
        use["rps"] = p.op("S", lambda S, o=scores[:, nt * 16:(nt + 1) * 16],
                          i=ps[5][:, 0:16]: S.activation(o, i, AF.Sigmoid),
                          waits=[mms[-1]])
    gz = p.op("V", lambda V, o=gsc[:, 4:8]: V.memset(o, -1e30))
    for nt in range(NT):
        sc_nt = scores[:, nt * 16:(nt + 1) * 16]
        s1 = p.op("V", lambda V, o=t16, a=sc_nt, b=ebb[:]:
                  V.tensor_tensor(o, a, b, ALU.add), waits=[use["rps"], eb2])
        pairs = [(0, 1), (0, 2), (0, 3), (1, 2), (1, 3), (2, 3)]
        ptoks = []
        for k2, (e1, e2) in enumerate(pairs):
            a_ap = bass.AP(t["t16"], e1, [[16, 128], [4, 4]])
            b_ap = bass.AP(t["t16"], e2, [[16, 128], [4, 4]])
            if k2 == 0:
                ptoks.append(p.op("V", lambda V, o=gsc[:, 0:4], a=a_ap, b=b_ap:
                                  V.tensor_tensor(o, a, b, ALU.add), waits=[s1]))
            else:
                pr = p.op("V", lambda V, o=t["pr4"][:], a=a_ap, b=b_ap:
                          V.tensor_tensor(o, a, b, ALU.add), waits=[s1, ptoks[-1]])
                ptoks.append(p.op("V", lambda V, o=gsc[:, 0:4], a=gsc[:, 0:4],
                                  b=t["pr4"][:]: V.tensor_tensor(o, a, b, ALU.max),
                                  waits=[pr]))
        g1 = p.op("V", lambda V, o=m8r, i=gsc: V.max(o, i), waits=[ptoks[-1], gz])
        g2 = p.op("V", lambda V, o=m8r[:, 2:8]: V.memset(o, -1e30), waits=[g1])
        g3 = p.op("V", lambda V, o=mr16[:, 0:8], r=m8r, i=gsc:
                  V.match_replace(o, r, i, -1e30), waits=[g2])
        g4 = p.op("V", lambda V, o=gm, a=gsc[:, 0:4], b=mr16[:, 0:4]:
                  V.tensor_tensor(o, a, b, ALU.not_equal), waits=[g3])
        g5 = p.op("V", lambda V, o=bass.AP(t["smask"], 0, [[16, 128], [4, 4], [1, 4]]),
                  i=bass.AP(t["gm"], 0, [[4, 128], [1, 4], [0, 4]]):
                  V.tensor_copy(o, i), waits=[g4])
        m1 = p.op("V", lambda V, o=t16b, a=t16, b=smask:
                  V.tensor_tensor(o, a, b, ALU.mult), waits=[g5])
        m2 = p.op("V", lambda V, o=m8r, i=t16b: V.max(o, i), waits=[m1])
        m3 = p.op("V", lambda V, o=m8r[:, 4:8]: V.memset(o, -1e30), waits=[m2])
        m4 = p.op("V", lambda V, o=mr16, r=m8r, i=t16b:
                  V.match_replace(o, r, i, -1e30), waits=[m3])
        m5 = p.op("V", lambda V, o=smask, a=t16b, b=mr16:
                  V.tensor_tensor(o, a, b, ALU.not_equal), waits=[m4])
        w1 = p.op("V", lambda V, o=wsel, a=smask, b=sc_nt:
                  V.tensor_tensor(o, a, b, ALU.mult), waits=[m5])
        w2 = p.op("V", lambda V, o=dn, i=wsel: V.reduce_sum(o, i, axis=AX.X),
                  waits=[w1])
        w3 = p.op("V", lambda V, o=dn, i=dn: V.tensor_scalar(
            o, i, 1e-20, None, ALU.add), waits=[w2])
        w4 = p.op("V", lambda V, o=dn, i=dn: V.reciprocal(o, i), waits=[w3])
        p.op("V", lambda V, o=comb[:, nt * 16:(nt + 1) * 16], a=wsel, s=dn:
             V.tensor_scalar(o, a, s, None, ALU.mult), waits=[w4])
    p.barrier()
    gp_sb, up_sb, dp_sb = t["gp_sb"], t["up_sb"], t["dp_sb"]
    acts = t["acts"]
    stmp = t["stmp"]
    use = {}
    st_e = {}

    def emit_loads(e):
        par = e % 2
        st_e[e] = s = {}
        s["lg"] = p.dma("P", gp_sb[par][:],
                        chunked_src(t["l1_gpT"][e], 0, EI, EI, IC),
                        waits=[use.get(("gp", par))])
        s["lu"] = p.dma("P", up_sb[par][:],
                        chunked_src(t["l1_upT"][e], 0, EI, EI, IC),
                        waits=[use.get(("up", par))])
        s["ld"] = p.dma("P", dp_sb[par][:],
                        chunked_src(t["l1_dpT"][e], 0, D, D, 4),
                        waits=[use.get(("dp", par))])

    def emit_gu_chunk(e, oc, nh):
        par = e % 2
        s = st_e[e]
        act = acts[par]
        gmm = []
        umm = []
        for j in range(2):
            for ic in range(IC):
                gmm.append(p.op("T", lambda T, o=ps[j][:, 0:512],
                    l=gp_sb[par][:, ic * 512 + oc * 128: ic * 512 + (oc + 1) * 128],
                    r=xnT[:, ic * N + nh * 1024 + j * 512: ic * N + nh * 1024 + (j + 1) * 512],
                    st=ic == 0, sp=ic == IC - 1:
                    T.matmul(o, l, r, start=st, stop=sp),
                    waits=([s["lg"], use.get(("gps", j))] if ic == 0 else [])))
            for ic in range(IC):
                umm.append(p.op("T", lambda T, o=ps[2 + j][:, 0:512],
                    l=up_sb[par][:, ic * 512 + oc * 128: ic * 512 + (oc + 1) * 128],
                    r=xnT[:, ic * N + nh * 1024 + j * 512: ic * N + nh * 1024 + (j + 1) * 512],
                    st=ic == 0, sp=ic == IC - 1:
                    T.matmul(o, l, r, start=st, stop=sp),
                    waits=([s["lu"], use.get(("ups", j))] if ic == 0 else [])))
        s["gu_last"] = umm[-1]
        for j in range(2):
            sl = p.op("S", lambda S, o=stmp[:, j * 512:(j + 1) * 512],
                      i=ps[j][:, 0:512]:
                      S.activation(o, i, AF.Silu),
                      waits=[gmm[8 * j + 7], umm[8 * j + 7],
                             use.get(("stmp", j))])
            av = p.op("V", lambda V,
                      o=act[:, oc * N + nh * 1024 + j * 512: oc * N + nh * 1024 + (j + 1) * 512],
                      a=stmp[:, j * 512:(j + 1) * 512], b=ps[2 + j][:, 0:512]:
                      V.tensor_tensor(o, a, b, ALU.mult),
                      waits=[sl, use.get(("actfree", par))])
            use[("gps", j)] = av
            use[("ups", j)] = av
            use[("stmp", j)] = av
            s.setdefault("act_toks", []).append(av)
        use[("gp", par)] = gmm[-1]
        use[("up", par)] = umm[-1]

    def emit_down_nt(e, nt):
        par = e % 2
        s = st_e[e]
        dmml = []
        for oc2 in range(2):
            mms = []
            for ic in range(4):
                mms.append(p.op("T", lambda T, o=ps[4 + oc2][:, 0:512],
                    l=acts[par][:, ic * N + nt * 128:ic * N + (nt + 1) * 128],
                    r=dp_sb[par][:, ic * D + oc2 * 512: ic * D + (oc2 + 1) * 512],
                    st=ic == 0, sp=ic == 3:
                    T.matmul(o, l, r, start=st, stop=sp),
                    waits=([s["ld"], use.get("dps")] + s["act_toks"]
                           if ic == 0 else [])))
            dmml.append(mms[-1])
        xa = t["x_res"][:, nt * D:(nt + 1) * D]
        ev = p.op("V", lambda V, o=xa, i=t["psbig"][:, 2048:3072],
                  s=comb[:, nt * 16 + e:nt * 16 + e + 1], x=xa:
                  V.scalar_tensor_tensor(o, i, s, x, ALU.mult, ALU.add),
                  waits=dmml + [use.get("xres")])
        use["dps"] = ev
        use["xres"] = ev
        use[("actfree", par)] = dmml[-1]
        use[("dp", par)] = dmml[-1]

    emit_loads(0)
    for oc in range(4):
        for nh in range(2):
            emit_gu_chunk(0, oc, nh)
    for e in range(1, NE):
        emit_loads(e)
        chunks = [(oc, nh) for oc in range(4) for nh in range(2)]
        for i, (oc, nh) in enumerate(chunks):
            emit_gu_chunk(e, oc, nh)
            for nt in (2 * i, 2 * i + 1):
                emit_down_nt(e - 1, nt)
    for nt in range(NT):
        emit_down_nt(NE - 1, nt)
    p.barrier()


def build_nc(stages=ALL_STAGES, dbg=()):
    nc = bass.Bass("TRN2", target_bir_lowering=False)
    t = {}
    for name, shape in WEIGHT_SPECS.items():
        t[name] = nc.dram_tensor(name, list(shape), I32 if name == "c_ridx" else F32,
                                 kind="ExternalInput").ap()
    for name, shape in BF16_INPUTS.items():
        t[name] = nc.dram_tensor(name, list(shape), BF16, kind="ExternalInput").ap()
    out = nc.dram_tensor("out", [N, D], F32, kind="ExternalOutput").ap()
    t["hbm_rows"] = [nc.dram_tensor(f"hbm_row{i}", [4095], BF16,
                                    kind="Internal").ap() for i in range(2)]
    dbg_out = {}
    for d in dbg:
        if d == "xnT":
            dbg_out[d] = nc.dram_tensor("dbg_xnT", [128, IC * N], BF16,
                                        kind="ExternalOutput").ap()
        if d == "att1":
            dbg_out["att1"] = nc.dram_tensor("dbg_att", [128, IC * N], BF16,
                                             kind="ExternalOutput").ap()
            dbg_out["strip1"] = nc.dram_tensor("dbg_strip", [128, 3968], BF16,
                                               kind="ExternalOutput").ap()
    t["dbg_out"] = dbg_out

    es = ExitStack()
    p = Prog(nc, es)

    namectr = {}

    def sb(name, shape, dtype, stack=es):
        n2 = namectr.get(name, 0)
        namectr[name] = n2 + 1
        uname = name if n2 == 0 else f"{name}__{n2}"
        h = stack.enter_context(nc.sbuf_tensor(uname, list(shape), dtype))
        t[name] = h
        return h

    sb("x_res", (128, NT * D), F32)
    sb("xnT", (128, IC * N), BF16)
    sb("dump", (128, 1024), F32)
    sb("ident", (128, 128), BF16)
    sb("tril", (128, 128), BF16)
    sb("ones", (1, 128), F32)
    sb("ones_bf", (1, 128), BF16)
    sb("ridx", (128, 1), I32)
    t["psbig"] = es.enter_context(nc.psum_tensor("psbig", [128, 3072], F32))
    t["ps"] = [t["psbig"][:, i * 512:(i + 1) * 512] for i in range(6)]
    t["pt"] = [es.enter_context(nc.psum_tensor(f"pt{i}", [128, 1024], BF16))
               for i in range(2)]

    p.dma("Y", t["x_res"][:], chunked_src(t["x"], 0, D, D, NT))
    p.dma("P", t["ident"][:], t["c_ident"])
    p.dma("P", t["tril"][:], t["c_tril"])
    p.dma("Y", t["ones"][:], t["c_ones"])
    p.dma("P", t["ones_bf"][:], t["c_ones"])
    p.dma("Y", t["ridx"][:], t["c_ridx"].rearrange("(p o) -> p o", o=1))
    p.barrier()

    for layer in (0, 1):
        lnst = "ln0" if layer == 0 else "ln1"
        if lnst in stages:
            with ExitStack() as lnes:
                sb("lnscratch", (128, 16), F32, lnes)
                sb("ln_xtile", (128, D), BF16, lnes)
                emit_ln(p, t, None, layer)
            if layer == 0 and "xnT" in dbg_out:
                p.dma("Y", dbg_out["xnT"], t["xnT"][:])
                p.barrier()
        atst = "attn0" if layer == 0 else "attn1"
        if atst in stages:
            with ExitStack() as pes:
                sb("qrot", (128, 2 * N), BF16, pes)
                sb("krot", (128, 2 * N), BF16, pes)
                sb("vg", (128, NT * 260), BF16, pes)
                sb("att", (128, IC * N), BF16, pes)
                sb("pbuf0", (128, 1024), BF16, pes)
                sb("pbuf1", (128, 1024), BF16, pes)
                sb("rr", (1, N), F32, pes)
                if layer == 1:
                    sb("bq8", (128, 8), F32, pes)
                    sb("bkt", (128, 8), F32, pes)
                    sb("bvrow", (1, D), BF16, pes)
                    sb("borow", (1, D), BF16, pes)
                    sb("relb", (1, 496), F32, pes)
                    sb("relb_rev", (1, 496), F32, pes)
                    sb("frrow", (1, 4095), BF16, pes)
                    t["strips"] = [pes.enter_context(nc.sbuf_tensor(
                        f"strip{i}", [128, 3968], BF16)) for i in range(2)]
                else:
                    t["bvrow"] = None
                with ExitStack() as pes2:
                    sb("wg", (128, IC * 256), BF16, pes2)
                    if layer == 0:
                        sb("qe", (128, N), BF16, pes2)
                        sb("qo", (128, N), BF16, pes2)
                        sb("tmpb", (128, N), BF16, pes2)
                        sb("cos", (128, N), BF16, pes2)
                        sb("sin", (128, N), BF16, pes2)
                        sb("qrot2", (128, 2 * N), BF16, pes2)
                        t["krot2"] = t["qrot"]
                        sb("pm_sb", (128, 4 * 128), BF16, pes2)
                        p.dma("P", t["cos"][:], t["c_cos"])
                        p.dma("P", t["sin"][:], t["c_sin"])
                        t["pm_tok"] = p.dma("P", t["pm_sb"][:], bass.AP(
                            t["c_perm"].tensor, 0,
                            [[128, 128], [128 * 128, 4], [1, 128]]))
                    else:
                        sb("lnc", (128, 2), F32, pes2)
                    emit_attn_core(p, t, None, layer)
                if layer == 1 and "att1" in dbg_out:
                    p.barrier("Y")
                    p.dma("Y", dbg_out["att1"], t["att"][:])
                    p.dma("Y", dbg_out["strip1"], t["strips"][1][:])
                    p.barrier()
                sb("wo_sb", (128, IC * 256), BF16, pes)
                emit_attn_proj(p, t, None, layer)
        if layer == 0 and "ffn0" in stages:
            with ExitStack() as pes:
                sb("gate", (128, 128), F32, pes)
                sb("idxI", (128, 128), I32, pes)
                sb("lin1", (128, 1), F32, pes)
                sb("lin2", (128, 1), F32, pes)
                sb("lg1", (128, 1), F32, pes)
                with ExitStack() as pes2:
                    sb("mxA", (128, 128), F32, pes2)
                    sb("mxB", (128, 128), F32, pes2)
                    sb("ixA", (128, 128), F32, pes2)
                    sb("ixB", (128, 128), F32, pes2)
                    sb("am", (128, 8), F32, pes2)
                    sb("ar", (128, 1), F32, pes2)
                    sb("keys_sb", (128, 8192), BF16, pes2)
                    t["wqf"] = [pes2.enter_context(
                        nc.sbuf_tensor(f"wqf{i}", [128, IC * 512], BF16))
                        for i in range(2)]
                    sb("qph", (128, 4 * N), BF16, pes2)
                    sb("sim_sb", (128, 128), F32, pes2)
                    sb("riota", (128, 128), F32, pes2)
                    t["riota_tok"] = p.dma("Y", t["riota"][:], t["c_riota"])
                    emit_ffn0_sel(p, t, None)
                sb("win", (128, IC * DI), BF16, pes)
                sb("wout", (128, 16 * D), BF16, pes)
                sb("gbuf0", (128, 3072), BF16, pes)
                sb("gbuf1", (128, 3072), BF16, pes)
                sb("hid0", (128, DI), F32, pes)
                sb("hid1", (128, DI), F32, pes)
                sb("hid_bf", (128, DI), BF16, pes)
                sb("hidT", (128, DI), BF16, pes)
                sb("xn_tile", (128, D), BF16, pes)
                emit_ffn0_main(p, t, None)
        if layer == 1 and "moe" in stages:
            with ExitStack() as pes:
                sb("gw_sb", (128, IC * 16), BF16, pes)
                sb("ebrow", (1, NE), F32, pes)
                sb("ebb", (128, NE), F32, pes)
                sb("scores", (128, NT * 16), F32, pes)
                sb("comb", (128, NT * 16), F32, pes)
                sb("m8r", (128, 8), F32, pes)
                sb("mr16", (128, 16), F32, pes)
                sb("gsc", (128, 8), F32, pes)
                sb("t16", (128, 16), F32, pes)
                sb("t16b", (128, 16), F32, pes)
                sb("gm", (128, 4), F32, pes)
                sb("pr4", (128, 4), F32, pes)
                sb("smask", (128, 16), F32, pes)
                sb("wsel", (128, 16), F32, pes)
                sb("dn", (128, 1), F32, pes)
                t["gp_sb"] = [pes.enter_context(
                    nc.sbuf_tensor(f"gp_sb{i}", [128, IC * 512], BF16))
                    for i in range(2)]
                t["up_sb"] = [pes.enter_context(
                    nc.sbuf_tensor(f"up_sb{i}", [128, IC * 512], BF16))
                    for i in range(2)]
                t["dp_sb"] = [pes.enter_context(
                    nc.sbuf_tensor(f"dp_sb{i}", [128, 4 * D], BF16))
                    for i in range(2)]
                t["acts"] = [pes.enter_context(nc.sbuf_tensor(
                    f"act{i}", [128, 4 * N], BF16)) for i in range(2)]
                sb("stmp", (128, 1024), F32, pes)
                emit_moe(p, t, None)

    p.barrier("Y")
    p.dma("Y", bass.AP(out.tensor, 0, [[D, 128], [128 * D, NT], [1, D]]),
          t["x_res"][:], sem="dO")
    p.cmds["Y"].append(lambda E, h=p.sem["dO"], v=p.cnt["dO"]: E.wait_ge(h, v))

    with nc.Block() as block:
        p.finalize(block)
    es.close()
    return nc


_CACHED = {}


def kernel(**inputs):
    x_full = inputs["L0_attn_x"]
    prep = host_prep(inputs)
    nc = build_nc()
    in_maps = []
    for b in range(8):
        m = dict(prep)
        m["x"] = np.ascontiguousarray(x_full[b])
        in_maps.append(m)
    res = run_bass_kernel_spmd(nc, in_maps, core_ids=list(range(8)))
    out = np.stack([res.results[b]["out"] for b in range(8)], axis=0)
    return out.astype(np.float32)
